# version 04
# baseline (speedup 1.0000x reference)
import numpy as np
from contextlib import ExitStack

import concourse.bass as bass
import concourse.mybir as mybir
from concourse.bass_utils import run_bass_kernel_spmd

F32 = mybir.dt.float32
BF16 = mybir.dt.bfloat16
AF = mybir.ActivationFunctionType
ALU = mybir.AluOpType
AX = mybir.AxisListType

D = 2048
TB = 2048
TOK = 1024
NJ = 16
NP = 8
NKT = 16
H = 16
HD = 128
NG = 4
FI = 1408
NIC = 11
NE = 8
NM = 16
EPS = 1e-5

ENG = {"t": "tensor", "v": "vector", "s": "scalar", "g": "gpsimd", "y": "sync"}
DBG = {}
STOP = None
MMAX = 16


class Pr:
    def __init__(self, nc, block, sems):
        self.nc, self.block, self.sems = nc, block, sems
        self.pend = {k: [] for k in ENG}
        self.cnt = {k: 0 for k in sems}
        self.lastw = {}

    def op(self, eng, fn, inc=None):
        amt = 0
        if inc is not None:
            amt = 16 if inc in ("dw", "dx") else 1
            self.cnt[inc] += amt
        sems = self.sems
        dr = eng == "v"

        def run(e, fn=fn, inc=inc, amt=amt, dr=dr):
            if dr:
                e.drain()
            r = fn(e)
            if inc is not None:
                r.then_inc(sems[inc], amt)

        self.pend[eng].append(run)
        return self.cnt[inc] if inc is not None else None

    def wait(self, eng, sem, val):
        if val is None or val <= 0:
            return
        key = (eng, sem)
        if self.lastw.get(key, 0) >= val:
            return
        self.lastw[key] = val
        sems = self.sems
        self.pend[eng].append(lambda e, sem=sem, val=val: e.wait_ge(sems[sem], val))

    def bar(self, eng, srcs=("sv", "ss")):
        for s_ in srcs:
            self.wait(eng, s_, self.cnt[s_])

    def flush(self):
        for eng, lst in self.pend.items():
            if not lst:
                continue

            def mk(lst):
                def f(e):
                    for fn in lst:
                        fn(e)
                return f

            getattr(self.block, ENG[eng])(mk(lst))
            self.pend[eng] = []


def build_program():
    nc = bass.Bass("TRN2", target_bir_lowering=False)
    dram = lambda n, s, ty=F32, k="ExternalInput": nc.dram_tensor(n, s, ty, kind=k).ap()

    xT = dram("xT", [D, TB])
    xown = dram("xown", [D, TOK])
    wq = dram("wq", [NG, 128, NJ, 512])
    wk = dram("wk", [NG, 128, NJ, 512])
    wv = dram("wv", [NG, 128, NJ, 512])
    wo = dram("wo", [NM, 128, NJ, 128])
    wmod = dram("wmod", [24, 128, NJ, 512])
    cbc = dram("cbc", [128, NJ])
    sgw = dram("sgw", [NIC, 128, NJ, 128])
    suw = dram("suw", [NIC, 128, NJ, 128])
    sdw = dram("sdw", [NM, 128, NIC, 128])
    wgw = dram("wgw", [NE, NIC, 128, NJ, 128])
    wuw = dram("wuw", [NE, NIC, 128, NJ, 128])
    wdw = dram("wdw", [NE, NM, 128, NIC, 128])
    rww = dram("rww", [128, NJ, 8])
    gamma4 = dram("gamma4", [128, 512])
    beta4 = dram("beta4", [128, 512])
    fmask = dram("fmask", [2, 128, 128])
    ident = dram("ident", [128, 128])
    out = dram("out", [D, TOK], F32, "ExternalOutput")
    x1s = dram("x1s", [D, TOK], F32, "Internal")
    dbg = {k: dram("dbg_" + k, list(s), F32, "ExternalOutput") for k, s in DBG.items()}

    es = ExitStack()
    sb = lambda n, s, ty: es.enter_context(nc.sbuf_tensor(n, s, ty))
    psal = lambda n: es.enter_context(nc.psum_tensor(n, [128, 1024], F32))

    gam4 = sb("gam4", [128, 512], BF16)
    bet4 = sb("bet4", [128, 512], F32)
    fmk = sb("fmk", [128, 2 * 128], BF16)
    idb = sb("idb", [128, 128], BF16)
    idf = sb("idf", [128, 128], F32)
    onesc = sb("onesc", [128, 1], BF16)
    onesr = sb("onesr", [1, 128], BF16)
    modc = sb("modc", [128, 96], F32)
    cbs = sb("cbs", [128, NJ], F32)
    silc = sb("silc", [128, NJ], BF16)
    mder = sb("mder", [128, 96], F32)
    mtmp = sb("mtmp", [128, 48], F32)
    rowA = sb("rowA", [32, 1024], F32)
    rowB = sb("rowB", [32, 1024], F32)
    fmst = sb("fmst", [128, 384], F32)
    junk = sb("junk", [128, 128], BF16)
    arow = sb("arow", [1, 512], BF16)
    brow = sb("brow", [1, 512], BF16)
    zrow = sb("zrow", [1, 1024], BF16)
    mrow = sb("mrow", [1, 1024], BF16)
    wmb = sb("wmb", [128, 1024], BF16)
    rwsb = sb("rwsb", [128, NJ * 8], BF16)
    wb = sb("wb", [128, 16384], BF16)
    wbf = wb[:].bitcast(F32)

    pO = psal("pO")
    pS0 = psal("pS0")
    pS1 = psal("pS1")
    pZ = psal("pZ")

    esH = ExitStack()
    h1 = esH.enter_context(nc.sbuf_tensor("h1", [128, NJ * TB], BF16))
    esA = ExitStack()
    sbA = lambda n, s, ty: esA.enter_context(nc.sbuf_tensor(n, s, ty))
    xbf = sbA("xbf", [128, NJ * TB], BF16)
    sct = sbA("sct", [128, 1024], BF16)
    abc1 = sbA("abc1", [128, TB], BF16)
    bbc1 = sbA("bbc1", [128, TB], BF16)

    sems = {k: es.enter_context(nc.semaphore(k)) for k in ("st", "sv", "ss", "dw", "dx")}
    block = es.enter_context(nc.Block())
    p = Pr(nc, block, sems)
    p._sig = None
    p._he = None
    p._trev = [None, None, None]
    p._sig2 = [None, None]
    p._he2 = [None, None]
    W, O = p.wait, p.op

    taps = {}

    def ckpt(name, tap_list):
        if STOP != name:
            return False
        for tag, ap in tap_list:
            t = nc.dram_tensor("tap_" + tag, [ap.shape[0], ap.shape[1]], ap.dtype,
                               kind="ExternalOutput").ap()
            p.bar("y", ("sv", "ss", "st"))
            O("y", lambda e, t=t, ap=ap: e.dma_start(out=t, in_=ap), "dx")
            taps[tag] = t
        W("y", "dx", p.cnt["dx"])
        p.flush()
        return True

    SA, CA, GA, SBv, CB, GB = range(6)
    mg1, mg2, mgac, mgbc, mgbg2, mgbs2 = range(6)
    msl = lambda i: mder[:, i * 16:(i + 1) * 16]

    c_bet4 = O("y", lambda e: e.dma_start(out=bet4[:], in_=beta4[:]), "dx")
    c_idf = O("y", lambda e: e.dma_start(out=idf[:], in_=ident[:]), "dx")
    c_cbs = O("y", lambda e: e.dma_start(out=cbs[:], in_=cbc[:]), "dx")
    c_gam = O("g", lambda e: e.dma_start(out=gam4[:], in_=gamma4[:]), "dw")
    c_idb = O("g", lambda e: e.dma_start(out=idb[:], in_=ident[:]), "dw")
    c_rw = O("g", lambda e: e.dma_start(out=rwsb[:], in_=rww[:].rearrange("p j e -> p (j e)")), "dw")
    c_fmk = None
    for m in range(2):
        c_fmk = O("g", lambda e, m=m: e.dma_start(out=fmk[:, m * 128:(m + 1) * 128], in_=fmask[m]), "dw")
    O("v", lambda e: e.memset(onesc[:], 1.0))
    O("v", lambda e: e.memset(onesr[:], 1.0))
    O("v", lambda e: e.memset(rowA[:], 0.0))
    O("v", lambda e: e.memset(rowB[:], 0.0), "sv")
    W("s", "dx", c_cbs)
    s_sil = O("s", lambda e: e.activation(silc[:], cbs[:], AF.Silu), "ss")

    dw_wm = [None] * 24
    dw_x = [None] * NJ
    wm_fin = [None] * 24

    def dma_wm(nb):
        slot = wb[:, (nb % 2) * 8192:(nb % 2) * 8192 + 8192]
        if nb >= 2:
            W("g", "st", wm_fin[nb - 2])
        dw_wm[nb] = O("g", lambda e, nb=nb, slot=slot: e.dma_start(
            out=slot, in_=wmod[nb].rearrange("p j c -> p (j c)")), "dw")

    dma_wm(0)
    dma_wm(1)
    for i in range(NJ):
        dw_x[i] = O("g", lambda e, i=i: e.dma_start(
            out=xbf[:, i * TB:(i + 1) * TB], in_=xT[i * 128:(i + 1) * 128, :]), "dw")

    xsq = sct
    mod_evac = [None] * 24
    col_fin = [None] * 24
    st_stats = None
    s_xsq = None

    def emit_mod_block(nb):
        slot = wb[:, (nb % 2) * 8192:(nb % 2) * 8192 + 8192]
        bank = pO if nb % 2 == 0 else pS0
        prow = bank[0:1, 0:512]
        W("t", "dw", dw_wm[nb])
        W("t", "ss", s_sil)
        if nb >= 2:
            W("t", "sv", mod_evac[nb - 2])
        for jc in range(NJ):
            wm_fin[nb] = O("t", lambda e, jc=jc, slot=slot, prow=prow: e.matmul(
                prow, silc[:, jc:jc + 1], slot[:, jc * 512:(jc + 1) * 512],
                start=(jc == 0), stop=(jc == NJ - 1)), "st")
        W("v", "st", wm_fin[nb])
        if nb >= 2:
            W("v", "st", col_fin[nb - 2])
        mod_evac[nb] = O("v", lambda e, nb=nb, prow=prow: e.tensor_copy(
            mrow[0:1, (nb % 2) * 512:(nb % 2) * 512 + 512], prow), "sv")
        W("t", "sv", mod_evac[nb])
        v_mcol = None
        for jcl in range(4):
            col = (nb // 4) * 16 + (nb % 4) * 4 + jcl
            pcol = bank[:, 520 + jcl:521 + jcl]
            if nb >= 2:
                W("t", "sv", mod_evac[nb - 2])
            col_fin[nb] = O("t", lambda e, nb=nb, jcl=jcl, pcol=pcol: e.matmul(
                pcol, mrow[0:1, (nb % 2) * 512 + jcl * 128:(nb % 2) * 512 + jcl * 128 + 128],
                onesr[0:1, 0:1], start=True, stop=True), "st")
            W("v", "st", col_fin[nb])
            v_mcol = O("v", lambda e, col=col, pcol=pcol: e.tensor_copy(
                modc[:, col:col + 1], pcol), "sv")
        mod_evac[nb] = v_mcol
        if nb + 2 < 8:
            dma_wm(nb + 2)

    def emit_mod_block_deferred(nb, dfer_state):
        par = nb % 2
        prow = pZ[32 + 32 * par:33 + 32 * par, 0:512]
        dw_j = dfer_state.setdefault("dw", {})
        fin_j = dfer_state.setdefault("fin", {})
        for jc in range(2):
            W("g", "st", fin_j.get((nb - 1, NJ - 2 + jc)))
            dw_j[(nb, jc)] = O("g", lambda e, nb=nb, jc=jc, par=par: e.dma_start(
                out=wmb[:, (jc % 2) * 512:(jc % 2) * 512 + 512],
                in_=wmod[nb, :, jc, :]), "dw")
        for jc in range(NJ):
            W("t", "dw", dw_j[(nb, jc)])
            fin_j[(nb, jc)] = O("t", lambda e, nb=nb, jc=jc: e.matmul(
                prow, silc[:, jc:jc + 1], wmb[:, (jc % 2) * 512:(jc % 2) * 512 + 512],
                start=(jc == 0), stop=(jc == NJ - 1)), "st")
            if jc + 2 < NJ:
                W("g", "st", fin_j[(nb, jc)])
                dw_j[(nb, jc + 2)] = O("g", lambda e, nb=nb, jc=jc, par=par: e.dma_start(
                    out=wmb[:, (jc % 2) * 512:(jc % 2) * 512 + 512],
                    in_=wmod[nb, :, jc + 2, :]), "dw")
        W("v", "st", fin_j[(nb, NJ - 1)])
        if nb >= 10:
            W("v", "st", col_fin[nb - 2])
        mod_evac[nb] = O("v", lambda e, nb=nb, prow=prow: e.tensor_copy(
            mrow[0:1, (nb % 2) * 512:(nb % 2) * 512 + 512], prow), "sv")
        W("t", "sv", mod_evac[nb])
        v_mcol = None
        for jcl in range(4):
            col = (nb // 4) * 16 + (nb % 4) * 4 + jcl
            pcol = pZ[:, 520 + 8 * par + jcl:521 + 8 * par + jcl]
            if nb >= 10:
                W("t", "sv", mod_evac[nb - 2])
            col_fin[nb] = O("t", lambda e, nb=nb, jcl=jcl, pcol=pcol: e.matmul(
                pcol, mrow[0:1, (nb % 2) * 512 + jcl * 128:(nb % 2) * 512 + jcl * 128 + 128],
                onesr[0:1, 0:1], start=True, stop=True), "st")
            W("v", "st", col_fin[nb])
            v_mcol = O("v", lambda e, col=col, pcol=pcol: e.tensor_copy(
                modc[:, col:col + 1], pcol), "sv")
        mod_evac[nb] = v_mcol

    for i in range(NJ):
        if i < 8:
            emit_mod_block(i)
        jc = i
        for hf in range(2):
            W("s", "dw", dw_x[jc])
            if s_xsq is not None:
                W("s", "st", st_stats)
            s_xsq = O("s", lambda e, jc=jc, hf=hf: e.activation(
                xsq[:], xbf[:, jc * TB + hf * 1024: jc * TB + (hf + 1) * 1024],
                AF.Square), "ss")
            W("t", "ss", s_xsq)
            W("t", "dw", dw_x[jc])
            for nt2 in range(2):
                r0 = 32 * nt2
                c0 = hf * 512
                off = jc * TB + hf * 1024 + nt2 * 512
                O("t", lambda e, off=off, r0=r0, c0=c0, jc=jc: e.matmul(
                    pZ[r0:r0 + 1, c0:c0 + 512], onesc[:], xbf[:, off:off + 512],
                    start=(jc == 0), stop=(jc == NJ - 1)), "st")
                st_stats = O("t", lambda e, nt2=nt2, r0=r0, c0=c0, jc=jc: e.matmul(
                    pS1[r0:r0 + 1, c0:c0 + 512], onesc[:], xsq[:, nt2 * 512:(nt2 + 1) * 512],
                    start=(jc == 0), stop=(jc == NJ - 1)), "st")

    def emit_clamp(src_i, dst):
        g_ = modc[:, src_i * 16:(src_i + 1) * 16]
        O("v", lambda e, g_=g_: e.tensor_scalar(mtmp[:, 0:16], g_, 0.0, None, ALU.is_ge))
        O("v", lambda e, g_=g_: e.tensor_scalar(mtmp[:, 16:32], g_, 0.25, None, ALU.max))
        O("v", lambda e, g_=g_: e.tensor_scalar(mtmp[:, 32:48], g_, -0.25, None, ALU.min))
        O("v", lambda e: e.tensor_tensor(mtmp[:, 16:32], mtmp[:, 16:32], mtmp[:, 32:48], ALU.subtract))
        O("v", lambda e: e.tensor_tensor(mtmp[:, 16:32], mtmp[:, 16:32], mtmp[:, 0:16], ALU.mult))
        return O("v", lambda e, dst=dst: e.tensor_tensor(msl(dst), mtmp[:, 16:32], mtmp[:, 32:48], ALU.add), "sv")

    W("v", "sv", mod_evac[7])
    v_mder = O("v", lambda e: e.tensor_scalar(msl(mg1), modc[:, CA * 16:(CA + 1) * 16], 1.0, None, ALU.add), "sv")

    def row_stats_to_ab(n_tok, st_after, sumap, sqap, adst, bdst, pbank):
        W("v", "st", st_after)
        sa = sb_ = None
        last_bc = [None, None]
        last_v = None
        for k in range(n_tok // 512):
            W("v", "ss", sb_)
            O("v", lambda e, k=k: e.tensor_copy(rowA[0:1, 0:512], sumap(k)))
            O("v", lambda e, k=k: e.tensor_copy(rowA[0:1, 512:1024], sqap(k)))
            O("v", lambda e: e.transpose(rowB[:, 0:1024], rowA[:, 0:1024]))
            ss_ = lambda c: rowB[:, c:512:32]
            qs_ = lambda c: rowB[:, 512 + c:1024:32]
            iD = 1.0 / D
            O("v", lambda e, ss_=ss_, iD=iD: e.tensor_scalar(ss_(2), ss_(0), iD, None, ALU.mult))
            O("v", lambda e, ss_=ss_, qs_=qs_, iD=iD: e.tensor_scalar(ss_(3), qs_(0), iD, None, ALU.mult))
            O("v", lambda e, ss_=ss_: e.tensor_tensor(ss_(4), ss_(2), ss_(2), ALU.mult))
            O("v", lambda e, ss_=ss_: e.tensor_tensor(ss_(5), ss_(3), ss_(4), ALU.subtract))
            O("v", lambda e, ss_=ss_: e.tensor_scalar(ss_(5), ss_(5), EPS, None, ALU.add))
            O("v", lambda e, ss_=ss_: e.reciprocal(ss_(6), ss_(5)))
            vs = O("v", lambda e: e.drain(), "sv")
            W("s", "sv", vs)
            s1 = O("s", lambda e, ss_=ss_: e.activation(ss_(7), ss_(6), AF.Sqrt), "ss")
            W("v", "ss", s1)
            O("v", lambda e, ss_=ss_, qs_=qs_: e.tensor_tensor(qs_(0), ss_(2), ss_(7), ALU.mult))
            O("v", lambda e, qs_=qs_: e.tensor_scalar(qs_(0), qs_(0), -1.0, None, ALU.mult))
            O("v", lambda e, ss_=ss_: e.tensor_scalar(ss_(0), ss_(7), 1.0, None, ALU.mult))
            v2 = O("v", lambda e: e.transpose(rowA[:, 0:1024], rowB[:, 0:1024]), "sv")
            W("s", "sv", v2)
            W("s", "st", last_bc[0])
            sa = O("s", lambda e: e.copy(arow[0:1, 0:512], rowA[0:1, 0:512]), "ss")
            sb_ = O("s", lambda e: e.copy(brow[0:1, 0:512], rowA[0:1, 512:1024]), "ss")
            W("t", "ss", sb_)
            for which, row_t, dst in ((0, arow, adst), (1, brow, bdst)):
                col = pbank[:, (which % 2) * 512:(which % 2) * 512 + 512]
                W("t", "sv", last_bc[1] if False else None)
                tm = O("t", lambda e, col=col, row_t=row_t: e.matmul(
                    col, onesr[:], row_t[0:1, 0:512], start=True, stop=True), "st")
                last_bc[0] = tm
                W("v", "st", tm)
                last_v = O("v", lambda e, col=col, dst=dst, k=k: e.tensor_copy(
                    dst[:, k * 512:(k + 1) * 512], col), "sv")
            W("t", "sv", last_v)
        return last_v

    W("t", "sv", mod_evac[7])
    vbb1 = row_stats_to_ab(
        TB, st_stats,
        lambda k: pZ[32 * (k % 2):32 * (k % 2) + 1, (k // 2) * 512:(k // 2) * 512 + 512],
        lambda k: pS1[32 * (k % 2):32 * (k % 2) + 1, (k // 2) * 512:(k // 2) * 512 + 512],
        abc1, bbc1, pO)

    W("v", "sv", vbb1)
    W("v", "sv", v_mder)
    dfer = {}
    h1_done = None
    for jc in range(NJ):
        W("v", "dw", dw_x[jc])
        for hf in range(2):
            off = jc * TB + hf * 1024
            O("v", lambda e, off=off, hf=hf: e.tensor_tensor(
                sct[:], xbf[:, off:off + 1024], abc1[:, hf * 1024:(hf + 1) * 1024], ALU.mult))
            O("v", lambda e, hf=hf: e.tensor_tensor(
                sct[:], sct[:], bbc1[:, hf * 1024:(hf + 1) * 1024], ALU.add))
            h1_done = O("v", lambda e, off=off, jc=jc: e.tensor_scalar(
                h1[:, off:off + 1024], sct[:],
                msl(mg1)[:, jc:jc + 1], modc[:, SA * 16 + jc:SA * 16 + jc + 1],
                ALU.mult, ALU.add), "sv")
    if ckpt("h1", [("h1", h1[:]), ("abc1", abc1[:]), ("bbc1", bbc1[:])]):
        return nc, taps
    p.flush()

    esA.close()
    esB = ExitStack()
    sbB = lambda n, s, ty: esB.enter_context(nc.sbuf_tensor(n, s, ty))
    osb = sbB("osb", [128, H * 1024], BF16)
    qg = sbB("qg", [128, NP * 512], BF16)
    kg = sbB("kg", [128, NKT * 512], BF16)
    vg = sbB("vg", [128, NKT * 512], BF16)
    ssq = sbB("ssq", [128, 2 * 1024], BF16)
    fqT = sbB("fqT", [128, 1024], BF16)
    fkT = sbB("fkT", [128, 2048], BF16)
    rz = sbB("rz", [128, 1024], BF16)

    QLO = [128 * (m // 2) for m in range(NKT)]
    wslot = [wb[:, 0:8192], wb[:, 8192:16384]]
    dw_w = [None, None]
    fin_w = [wm_fin[22], wm_fin[23]]
    slot_i = [0]

    def load_w(src_ap, scale_gamma):
        s_ = slot_i[0] % 2
        slot_i[0] += 1
        W("g", "st", fin_w[s_])
        dw_w[s_] = O("g", lambda e, s_=s_, src_ap=src_ap: e.dma_start(out=wslot[s_], in_=src_ap), "dw")
        vfin = None
        if scale_gamma:
            W("v", "dw", dw_w[s_])
            W("v", "dw", c_gam)
            for jc in range(NJ):
                vfin = O("v", lambda e, s_=s_, jc=jc: e.tensor_tensor(
                    wslot[s_][:, jc * 512:(jc + 1) * 512], wslot[s_][:, jc * 512:(jc + 1) * 512],
                    gam4[:], ALU.mult), "sv")
        return s_, vfin

    sQ, vQ = load_w(wq[0].rearrange("p j c -> p (j c)"), True)
    PSL = [(pO, 0), (pO, 512), (pS0, 0), (pS0, 512), (pS1, 0), (pS1, 512)]
    pi = [0]
    ev_ps = [vbb1, vbb1, None, None, None, None]

    def pslice():
        i = pi[0] % 6
        pi[0] += 1
        bank, off = PSL[i]
        return bank[:, off:off + 512], i

    st_group_end = None
    last_head_v = None
    iHD = 1.0 / HD
    MU, RS = 192, 288

    def fm_batch(c0, tiles, ss_after):
        sl_ = lambda b: fmst[:, b + c0:b + c0 + 16]
        W("v", "ss", ss_after)
        O("v", lambda e, sl_=sl_: e.tensor_scalar(sl_(MU), sl_(0), iHD, None, ALU.mult))
        O("v", lambda e, sl_=sl_: e.tensor_scalar(sl_(RS), sl_(96), iHD, None, ALU.mult))
        O("v", lambda e, sl_=sl_: e.tensor_tensor(sl_(0), sl_(MU), sl_(MU), ALU.mult))
        O("v", lambda e, sl_=sl_: e.tensor_tensor(sl_(RS), sl_(RS), sl_(0), ALU.subtract))
        O("v", lambda e, sl_=sl_: e.tensor_scalar(sl_(RS), sl_(RS), EPS, None, ALU.add))
        O("v", lambda e, sl_=sl_: e.reciprocal(sl_(96), sl_(RS)))
        vrc = O("v", lambda e: e.nop(), "sv")
        W("s", "sv", vrc)
        srs = O("s", lambda e, sl_=sl_: e.activation(sl_(RS), sl_(96), AF.Sqrt), "ss")
        W("v", "ss", srs)
        if c0 < 32:
            O("v", lambda e, sl_=sl_: e.tensor_scalar(sl_(RS), sl_(RS), HD ** -0.5, None, ALU.mult))
        vfm = None
        for dst, base, soff in tiles:
            for hh in range(4):
                vfm = O("v", lambda e, dst=dst, base=base, soff=soff, hh=hh: e.tensor_scalar(
                    dst[:, base + hh * 128: base + hh * 128 + 128],
                    dst[:, base + hh * 128: base + hh * 128 + 128],
                    fmst[:, MU + soff + hh:MU + soff + hh + 1],
                    fmst[:, RS + soff + hh:RS + soff + hh + 1],
                    ALU.subtract, ALU.mult), "sv")
        return vfm


    for g in range(NG):
        for nbd in range(8 + 4 * g, 12 + 4 * g):
            emit_mod_block_deferred(nbd, dfer)
        vfm = None
        sK = vK = sV = vV = None
        sK, vK = load_w(wk[g].rearrange("p j c -> p (j c)"), True)
        for kind in ("q", "k", "v"):
            if kind == "q":
                s_, vfin, ntile, dst, coloff = sQ, vQ, NP, qg, 256
            elif kind == "k":
                s_, vfin, ntile, dst, coloff = sK, vK, NKT, kg, 128
            else:
                s_, vfin, ntile, dst, coloff = sV, vV, NKT, vg, 128
            W("t", "dw", dw_w[s_])
            W("t", "sv", vfin)
            W("t", "sv", h1_done)
            for t_ in range(ntile):
                sl, i6 = pslice()
                evp = ev_ps[i6]
                if isinstance(evp, tuple):
                    W("t", evp[0], evp[1])
                else:
                    W("t", "sv", evp)
                W("t", "ss", p.cnt["ss"] if g > 0 and t_ == 0 and kind == "q" else None)
                for jc in range(NJ):
                    tm = O("t", lambda e, t_=t_, jc=jc, sl=sl, s_=s_, coloff=coloff: e.matmul(
                        sl, h1[:, jc * TB + coloff * t_: jc * TB + coloff * t_ + 128],
                        wslot[s_][:, jc * 512:(jc + 1) * 512],
                        start=(jc == 0), stop=(jc == NJ - 1)), "st")
                fin_w[s_] = tm
                W("v", "st", tm)
                W("v", "st", st_group_end)
                if kind == "v":
                    W("s", "st", tm)
                    W("s", "st", st_group_end)
                    ev = O("s", lambda e, sl=sl, t_=t_: e.activation(
                        vg[:, t_ * 512:(t_ + 1) * 512], sl, AF.Copy), "ss")
                else:
                    W("v", "dx", c_bet4)
                    ev = O("v", lambda e, sl=sl, dst=dst, t_=t_: e.tensor_tensor(
                        dst[:, t_ * 512:(t_ + 1) * 512], sl, bet4[:], ALU.add), "sv")
                ev_ps[i6] = ("ss", ev) if kind == "v" else ev
                if kind != "v":
                    soff = (t_ * 4) if kind == "q" else (32 + t_ * 4)
                    O("v", lambda e, dst=dst, t_=t_, soff=soff: e.tensor_reduce(
                        fmst[:, soff:soff + 4],
                        dst[:, t_ * 512:(t_ + 1) * 512].rearrange("p (h d) -> p h d", d=128),
                        AX.X, ALU.add))
                    W("s", "sv", ev)
                    s_sq = None
                    for hh in range(4):
                        s_sq = O("s", lambda e, dst=dst, t_=t_, hh=hh, soff=soff: e.activation(
                            junk[:, 0:128],
                            dst[:, t_ * 512 + hh * 128: t_ * 512 + hh * 128 + 128],
                            AF.Square, accum_out=fmst[:, 96 + soff + hh:96 + soff + hh + 1]),
                            "ss" if hh == 3 else None)
                    if t_ % 4 == 3:
                        c0 = soff - 12
                        tiles = [(dst, (t_ - 3 + j) * 512, c0 + 4 * j) for j in range(4)]
                        vfm = fm_batch(c0, tiles, s_sq)
            if kind == "q":
                sV, vV = load_w(wv[g].rearrange("p j c -> p (j c)"), False)
            elif kind == "k" and g < NG - 1:
                sQ, vQ = load_w(wq[g + 1].rearrange("p j c -> p (j c)"), True)

        sqm = [None, None]
        avf = [None, None]
        st_scores_last = None
        if g == 0 and ckpt("fm0", [("qg", qg[:]), ("kg", kg[:]), ("vg", vg[:]),
                                   ("fmst", fmst[:])]):
            return nc, taps
        W("t", "ss", p.cnt["ss"])
        W("t", "sv", mod_evac[11 + 4 * g])
        for hh in range(4):
            h_ = g * 4 + hh
            W("t", "sv", vfm)
            W("t", "dw", c_idb)
            W("t", "ss", sqm[0])
            W("t", "ss", sqm[1])
            vtr = None
            for grp8 in range(3):
                bank, boff = ((pS0, 0), (pS1, 0), (pS0, 1024))[grp8]
                W("t", "sv", p._trev[grp8])
                ttr = None
                for j in range(8):
                    t_ = grp8 * 8 + j
                    src, base = (qg, t_ * 512) if t_ < NP else (kg, (t_ - NP) * 512)
                    sl = bank[:].bitcast(BF16)[:, boff + j * 128:boff + j * 128 + 128]
                    ttr = O("t", lambda e, src=src, base=base, hh=hh, sl=sl: e.transpose(
                        sl, src[:, base + hh * 128: base + hh * 128 + 128], idb[:]), "st")
                W("v", "st", ttr)
                W("v", "st", st_scores_last)
                dst2, doff = ((fqT, 0), (fkT, 0), (fkT, 1024))[grp8]
                full = bank[:].bitcast(BF16)[:, boff:boff + 1024]
                vtr = p._trev[grp8] = O("v", lambda e, full=full, dst2=dst2, doff=doff: e.tensor_copy(
                    dst2[:, doff:doff + 1024], full), "sv")
            if g == 0 and hh == 0 and ckpt("tr0", [("fqT", fqT[:]), ("fkT", fkT[:])]):
                return nc, taps
            for m in range(MMAX):
                slot = m % 2
                pS = (pS0, pS1)[slot]
                lo = QLO[m]
                W("t", "sv", vtr)
                W("t", "ss", sqm[slot])
                blocks = [(lo, 512), (512, 1024)] if lo < 512 else [(lo, 1024)]
                tb_ = None
                for (b0, b1) in blocks:
                    tb_ = O("t", lambda e, pS=pS, b0=b0, b1=b1, m=m: e.matmul(
                        pS[:, b0:b1], fkT[:, m * 128:(m + 1) * 128], fqT[:, b0:b1],
                        start=True, stop=True), "st")
                st_scores_last = tb_
                W("s", "st", tb_)
                W("s", "st", avf[slot])
                sqm[slot] = O("s", lambda e, pS=pS, lo=lo, slot=slot: e.activation(
                    ssq[:, slot * 1024 + lo: slot * 1024 + 1024], pS[:, lo:1024], AF.Square), "ss")
                pf = m // 2
                W("v", "ss", sqm[slot])
                W("v", "dw", c_fmk)
                vmk = O("v", lambda e, slot=slot, pf=pf, m=m: e.tensor_tensor(
                    ssq[:, slot * 1024 + pf * 128: slot * 1024 + pf * 128 + 128],
                    ssq[:, slot * 1024 + pf * 128: slot * 1024 + pf * 128 + 128],
                    fmk[:, (m % 2) * 128:(m % 2) * 128 + 128], ALU.mult), "sv")
                W("t", "ss", sqm[slot])
                for (b0, b1) in blocks:
                    if b0 <= 128 * pf < b1:
                        W("t", "sv", vmk)
                    O("t", lambda e, m=m, hh=hh, slot=slot, b0=b0, b1=b1: e.matmul(
                        pO[:, b0:b1], vg[:, m * 512 + hh * 128: m * 512 + hh * 128 + 128],
                        ssq[:, slot * 1024 + b0: slot * 1024 + b1],
                        start=(m == 0), stop=(m == (7 if b1 <= 512 else NKT - 1))), "st")
                    avf[slot] = O("t", lambda e, m=m, slot=slot, b0=b0, b1=b1: e.matmul(
                        pZ[0:1, b0:b1], onesc[:],
                        ssq[:, slot * 1024 + b0: slot * 1024 + b1],
                        start=(m == 0), stop=(m == (7 if b1 <= 512 else NKT - 1))), "st")
            if g == 0 and hh == 0 and ckpt("av0", [("ssq", ssq[:])]):
                return nc, taps
            W("v", "st", avf[0])
            W("v", "st", avf[1])
            O("v", lambda e: e.tensor_copy(rowA[0:1, 0:512], pZ[0:1, 0:512]))
            O("v", lambda e: e.tensor_copy(rowA[0:1, 512:1024], pZ[0:1, 512:1024]))
            O("v", lambda e: e.drain())
            O("v", lambda e: e.transpose(rowB[:, 0:1024], rowA[:, 0:1024]))
            zsl = lambda c: rowB[:, c:1024:32]
            O("v", lambda e, zsl=zsl: e.tensor_scalar(zsl(1), zsl(0), EPS, None, ALU.add))
            O("v", lambda e, zsl=zsl: e.reciprocal(zsl(0), zsl(1)))
            O("v", lambda e: e.drain())
            O("v", lambda e: e.transpose(rowA[:, 0:1024], rowB[:, 0:1024]))
            vz = O("v", lambda e: e.tensor_copy(zrow[0:1, 0:1024], rowA[0:1, 0:1024]), "sv")
            W("t", "sv", vz)
            W("t", "sv", last_head_v)
            for nt in range(2):
                tm = O("t", lambda e, nt=nt: e.matmul(
                    pZ[:, nt * 512:(nt + 1) * 512], onesr[:], zrow[0:1, nt * 512:(nt + 1) * 512],
                    start=True, stop=True), "st")
                W("v", "st", tm)
                O("v", lambda e, nt=nt: e.tensor_copy(rz[:, nt * 512:(nt + 1) * 512], pZ[:, nt * 512:(nt + 1) * 512]))
            for nt in range(2):
                last_head_v = O("v", lambda e, nt=nt, h_=h_: e.tensor_tensor(
                    osb[:, h_ * 1024 + nt * 512: h_ * 1024 + (nt + 1) * 512],
                    pO[:, nt * 512:(nt + 1) * 512], rz[:, nt * 512:(nt + 1) * 512], ALU.mult), "sv")
            W("t", "sv", last_head_v)
            if g == 0 and hh == 0 and ckpt("head0", [("osb0", osb[:, 0:1024]), ("fqT", fqT[:]), ("fkT", fkT[:]), ("rz", rz[:])]):
                return nc, taps
        st_group_end = p.cnt["st"]
        ev_ps[:] = [last_head_v] * 6
        p.flush()

    W("v", "sv", mod_evac[11])
    v_gac = emit_clamp(GA, mgac)
    woslot = [wb[:, 0:2048], wb[:, 2048:4096]]
    x1slot = [wbf[:, 4096:5120], wbf[:, 5120:6144]]
    dw_wo = [None, None]
    fin_wo = [fin_w[0], fin_w[1]]
    dx_x1 = [None, None]
    v_x1 = [None, None]
    for mc in range(NM):
        s_ = mc % 2
        W("g", "st", fin_wo[s_])
        dw_wo[s_] = O("g", lambda e, mc=mc, s_=s_: e.dma_start(
            out=woslot[s_], in_=wo[mc].rearrange("p j c -> p (j c)")), "dw")
        W("t", "dw", dw_wo[s_])
        W("t", "sv", last_head_v)
        for nt in range(2):
            sl, i6 = pslice()
            evp = ev_ps[i6]
            if isinstance(evp, tuple):
                W("t", evp[0], evp[1])
            else:
                W("t", "sv", evp)
            W("t", "ss", p.cnt["ss"] if mc == 0 and nt == 0 else None)
            for ic in range(NJ):
                tm = O("t", lambda e, s_=s_, ic=ic, nt=nt, sl=sl: e.matmul(
                    sl, woslot[s_][:, ic * 128:(ic + 1) * 128],
                    osb[:, ic * 1024 + nt * 512: ic * 1024 + (nt + 1) * 512],
                    start=(ic == 0), stop=(ic == NJ - 1)), "st")
            fin_wo[s_] = tm
            W("v", "st", tm)
            W("v", "dx", dx_x1[s_])
            W("v", "sv", v_gac)
            ev = O("v", lambda e, sl=sl, s_=s_, nt=nt, mc=mc: e.tensor_scalar(
                x1slot[s_][:, nt * 512:(nt + 1) * 512], sl,
                msl(mgac)[:, mc:mc + 1], None, ALU.mult), "sv")
            ev_ps[i6] = ev
        W("y", "sv", ev)
        dx_x1[s_] = O("y", lambda e, mc=mc, s_=s_: e.dma_start(
            out=x1s[mc * 128:(mc + 1) * 128, :], in_=x1slot[s_][:, 0:1024]), "dx")
    dx_x1_all = p.cnt["dx"]
    if ckpt("oproj", [("osb", osb[:])]):
        return nc, taps
    p.flush()

    esB.close()
    esH.close()
    esC = ExitStack()
    sbC = lambda n, s, ty: esC.enter_context(nc.sbuf_tensor(n, s, ty))
    acc = sbC("acc", [128, NM * 1024], F32)
    h2 = sbC("h2", [128, NJ * 1024], BF16)
    hebuf = sbC("hebuf", [128, NIC * 1024], BF16)
    sgact = sbC("sgact", [128, 1024], F32)
    tf1 = sbC("tf1", [128, 1024], F32)
    abc2 = sbC("abc2", [128, 1024], F32)
    bbc2 = sbC("bbc2", [128, 1024], F32)
    ltok = sbC("ltok", [128, 64], F32)
    etok = sbC("etok", [128, 64], F32)
    ctok = sbC("ctok", [128, 64], F32)
    lgr = rowA[0:8, 0:1024]
    crow = rowB[0:8, 0:1024]
    stok = sbC("stok", [128, 64], F32)
    cbcst = sbC("cbcst", [128, 1024], BF16)

    W("v", "sv", mod_evac[23])
    O("v", lambda e: e.tensor_scalar(msl(mg2), modc[:, CB * 16:(CB + 1) * 16], 1.0, None, ALU.add))
    emit_clamp(GB, mgbc)
    O("v", lambda e: e.tensor_tensor(msl(mgbg2), msl(mgbc), msl(mg2), ALU.mult))
    O("v", lambda e: e.tensor_tensor(msl(mgbs2), msl(mgbc), modc[:, SBv * 16:(SBv + 1) * 16], ALU.mult), "sv")

    xoslot = [wbf[:, 0:1024], wbf[:, 1024:2048]]
    x1ld = [wbf[:, 2048:3072], wbf[:, 3072:4096]]
    acc_bf = hebuf[:, 0:1024]
    acc_sq = hebuf[:, 1024:2048]
    v_acc = [None, None]
    st_l2 = None
    s_l2 = None
    W("y", "dx", dx_x1_all)
    for mc in range(NM):
        s_ = mc % 2
        W("y", "sv", v_acc[s_])
        W("y", "st", fin_wo[s_] if mc < 2 else None)
        O("y", lambda e, mc=mc, s_=s_: e.dma_start(
            out=xoslot[s_][:, 0:1024], in_=xown[mc * 128:(mc + 1) * 128, :]), "dx")
        dxv = O("y", lambda e, mc=mc, s_=s_: e.dma_start(
            out=x1ld[s_][:, 0:1024], in_=x1s[mc * 128:(mc + 1) * 128, :]), "dx")
        W("v", "dx", dxv)
        v_acc[s_] = O("v", lambda e, mc=mc, s_=s_: e.tensor_tensor(
            acc[:, mc * 1024:(mc + 1) * 1024], xoslot[s_][:, 0:1024], x1ld[s_][:, 0:1024], ALU.add), "sv")
        W("s", "sv", v_acc[s_])
        W("s", "st", st_l2)
        O("s", lambda e, mc=mc: e.activation(acc_bf, acc[:, mc * 1024:(mc + 1) * 1024], AF.Copy))
        s_l2 = O("s", lambda e, mc=mc: e.activation(acc_sq, acc[:, mc * 1024:(mc + 1) * 1024], AF.Square), "ss")
        W("t", "ss", s_l2)
        for nt in range(2):
            r0 = 32 * nt
            O("t", lambda e, nt=nt, r0=r0, mc=mc: e.matmul(
                pZ[r0:r0 + 1, 0:512], onesc[:], acc_bf[:, nt * 512:(nt + 1) * 512],
                start=(mc == 0), stop=(mc == NM - 1)), "st")
            st_l2 = O("t", lambda e, nt=nt, r0=r0, mc=mc: e.matmul(
                pZ[r0:r0 + 1, 512:1024], onesc[:], acc_sq[:, nt * 512:(nt + 1) * 512],
                start=(mc == 0), stop=(mc == NM - 1)), "st")

    p.bar("t")
    vbb2 = row_stats_to_ab(
        TOK, st_l2,
        lambda k: pZ[32 * k:32 * k + 1, 0:512],
        lambda k: pZ[32 * k:32 * k + 1, 512:1024],
        abc2, bbc2, pS0)
    ev_ps[2] = ev_ps[3] = vbb2

    if ckpt("ln2r", [("acc", acc[:])]):
        return nc, taps

    h2_ms = [None] * NM
    W("v", "sv", vbb2)
    h2g_last = None
    for mc in range(NM):
        O("v", lambda e, mc=mc: e.tensor_tensor(tf1[:], acc[:, mc * 1024:(mc + 1) * 1024], abc2[:], ALU.mult))
        O("v", lambda e: e.tensor_tensor(tf1[:], tf1[:], bbc2[:], ALU.add))
        O("v", lambda e, mc=mc: e.scalar_tensor_tensor(
            acc[:, mc * 1024:(mc + 1) * 1024], tf1[:], msl(mgbg2)[:, mc:mc + 1],
            acc[:, mc * 1024:(mc + 1) * 1024], ALU.mult, ALU.add))
        O("v", lambda e, mc=mc: e.tensor_scalar(
            acc[:, mc * 1024:(mc + 1) * 1024], acc[:, mc * 1024:(mc + 1) * 1024],
            msl(mgbs2)[:, mc:mc + 1], None, ALU.add))
        h2g_last = h2_ms[mc] = O("v", lambda e, mc=mc: e.tensor_scalar(
            h2[:, mc * 1024:(mc + 1) * 1024], tf1[:],
            msl(mg2)[:, mc:mc + 1], modc[:, SBv * 16 + mc:SBv * 16 + mc + 1],
            ALU.mult, ALU.add), "sv")
    p.flush()

    if ckpt("h2", [("h2", h2[:]), ("acc2", acc[:])]):
        return nc, taps

    W("t", "dw", c_rw)
    st_lg = None
    for jc in range(NJ):
        W("t", "sv", h2_ms[jc])
        for nt in range(2):
            st_lg = O("t", lambda e, nt=nt, jc=jc: e.matmul(
                pZ[0:8, nt * 512:(nt + 1) * 512], rwsb[:, jc * 8:(jc + 1) * 8],
                h2[:, jc * 1024 + nt * 512: jc * 1024 + (nt + 1) * 512],
                start=(jc == 0), stop=(jc == NJ - 1)), "st")
    W("v", "st", st_lg)
    v_lg = None
    for nt in range(2):
        v_lg = O("v", lambda e, nt=nt: e.tensor_copy(
            lgr[:, nt * 512:(nt + 1) * 512], pZ[0:8, nt * 512:(nt + 1) * 512]), "sv")
    W("t", "sv", v_lg)
    W("t", "dx", c_idf)
    v_lt = None
    v_lt_slot = [None, None]
    for tt in range(8):
        sl = pZ[:, 512 * (tt % 2):512 * (tt % 2) + 8]
        W("t", "sv", v_lt_slot[tt % 2])
        ttr = O("t", lambda e, tt=tt, sl=sl: e.transpose(sl, lgr[:, tt * 128:(tt + 1) * 128], idf[0:8, 0:8]), "st")
        W("v", "st", ttr)
        v_lt = v_lt_slot[tt % 2] = O("v", lambda e, tt=tt, sl=sl: e.tensor_copy(ltok[:, tt * 8:(tt + 1) * 8], sl), "sv")
    O("v", lambda e: e.tensor_reduce(stok[:, 0:8], ltok[:].rearrange("p (t e) -> p t e", e=8), AX.X, ALU.max))
    v_ng = O("v", lambda e: e.tensor_scalar(stok[:, 8:16], stok[:, 0:8], -1.0, None, ALU.mult), "sv")
    W("s", "sv", v_ng)
    s_ex = None
    for tt in range(8):
        s_ex = O("s", lambda e, tt=tt: e.activation(
            etok[:, tt * 8:(tt + 1) * 8], ltok[:, tt * 8:(tt + 1) * 8], AF.Exp,
            bias=stok[:, 8 + tt:8 + tt + 1]), "ss")
    W("v", "ss", s_ex)
    O("v", lambda e: e.tensor_reduce(stok[:, 16:24], etok[:].rearrange("p (t e) -> p t e", e=8), AX.X, ALU.add))
    O("v", lambda e: e.tensor_reduce(stok[:, 24:32], etok[:].rearrange("p (t e) -> p t e", e=8), AX.X, ALU.max))
    O("v", lambda e: e.reciprocal(stok[:, 40:48], stok[:, 16:24]))
    v_c = None
    for tt in range(8):
        esl = etok[:, tt * 8:(tt + 1) * 8]
        O("v", lambda e, esl=esl, tt=tt: e.tensor_scalar(mtmp[:, 0:8], esl, stok[:, 24 + tt:25 + tt], None, ALU.is_equal))
        O("v", lambda e, esl=esl: e.scalar_tensor_tensor(mtmp[:, 8:16], mtmp[:, 0:8], -1e30, esl, ALU.mult, ALU.add))
        O("v", lambda e, tt=tt: e.tensor_reduce(stok[:, 32 + tt:33 + tt], mtmp[:, 8:16], AX.X, ALU.max))
        O("v", lambda e, esl=esl, tt=tt: e.tensor_scalar(mtmp[:, 16:24], esl, stok[:, 32 + tt:33 + tt], None, ALU.is_ge))
        O("v", lambda e, esl=esl, tt=tt: e.tensor_scalar(mtmp[:, 24:32], esl, stok[:, 40 + tt:41 + tt], None, ALU.mult))
        v_c = O("v", lambda e, tt=tt: e.tensor_tensor(
            ctok[:, tt * 8:(tt + 1) * 8], mtmp[:, 24:32], mtmp[:, 16:24], ALU.mult), "sv")
    W("t", "sv", v_c)
    v_cr = None
    v_cr_slot = [None, None]
    for tt in range(8):
        sl = pZ[0:8, 512 * (tt % 2) + 128:512 * (tt % 2) + 256]
        W("t", "sv", v_cr_slot[tt % 2])
        ttr = O("t", lambda e, tt=tt, sl=sl: e.transpose(sl, ctok[:, tt * 8:(tt + 1) * 8], idf[:]), "st")
        W("v", "st", ttr)
        v_cr = v_cr_slot[tt % 2] = O("v", lambda e, tt=tt, sl=sl: e.tensor_copy(crow[:, tt * 128:(tt + 1) * 128], sl), "sv")
    p.flush()

    if ckpt("router", [("crow", crow[:, :]), ("ctok", ctok[:]), ("etok", etok[:])]):
        return nc, taps

    GU = [wb[:, 0:2048], wb[:, 2048:4096], wb[:, 4096:6144], wb[:, 6144:8192]]
    WD = [wb[:, 8192:9600], wb[:, 9600:11008]]
    dw_gu = [None, None]
    fin_gu = [None, None]
    dw_wd = [None, None]
    fin_wd = [None, None]

    def load_gu(pair, gsrc, usrc, waitfin):
        W("g", "st", waitfin)
        a = O("g", lambda e, pair=pair, gsrc=gsrc: e.dma_start(
            out=GU[2 * pair], in_=gsrc.rearrange("p j c -> p (j c)")), "dw")
        b = O("g", lambda e, pair=pair, usrc=usrc: e.dma_start(
            out=GU[2 * pair + 1], in_=usrc.rearrange("p j c -> p (j c)")), "dw")
        dw_gu[pair] = b
        return b

    def gate_up_act(e_idx, ic, pair, comb):
        par = ic % 2
        pG = (pS0, pO)[par]
        pU = (pS1, pZ)[par]
        W("t", "dw", dw_gu[pair])
        W("t", "ss", p._sig2[par])
        tg = tu = None
        for nt in range(2):
            for jc in range(NJ):
                tg = O("t", lambda e, nt=nt, jc=jc, pair=pair, pG=pG: e.matmul(
                    pG[:, nt * 512:(nt + 1) * 512], GU[2 * pair][:, jc * 128:(jc + 1) * 128],
                    h2[:, jc * 1024 + nt * 512: jc * 1024 + (nt + 1) * 512],
                    start=(jc == 0), stop=(jc == NJ - 1)), "st")
        W("t", "sv", p._he2[par])
        for nt in range(2):
            for jc in range(NJ):
                tu = O("t", lambda e, nt=nt, jc=jc, pair=pair, pU=pU: e.matmul(
                    pU[:, nt * 512:(nt + 1) * 512], GU[2 * pair + 1][:, jc * 128:(jc + 1) * 128],
                    h2[:, jc * 1024 + nt * 512: jc * 1024 + (nt + 1) * 512],
                    start=(jc == 0), stop=(jc == NJ - 1)), "st")
        fin_gu[pair] = tu
        W("s", "st", tg)
        W("s", "sv", p._he)
        p._sig = p._sig2[par] = O("s", lambda e, pG=pG: e.activation(
            sgact[:], pG[:, 0:1024], AF.Sigmoid), "ss")
        W("v", "ss", p._sig)
        W("v", "st", tu)
        he = O("v", lambda e, ic=ic, pU=pU: e.scalar_tensor_tensor(
            hebuf[:, ic * 1024:(ic + 1) * 1024], pU[:, 0:1024], 1.0, sgact[:],
            ALU.mult, ALU.mult), "sv")
        if comb is not None:
            he = O("v", lambda e, ic=ic: e.tensor_tensor(
                hebuf[:, ic * 1024:(ic + 1) * 1024], hebuf[:, ic * 1024:(ic + 1) * 1024],
                cbcst[:], ALU.mult), "sv")
        p._he = p._he2[par] = he
        return he

    def down_proj(wd_src_fn, he_ready, tag):
        last_ev = None
        for mc in range(NM):
            s_ = mc % 2
            W("g", "st", fin_wd[s_])
            dw_wd[s_] = O("g", lambda e, s_=s_, mc=mc: e.dma_start(
                out=WD[s_], in_=wd_src_fn(mc).rearrange("p j c -> p (j c)")), "dw")
            W("t", "dw", dw_wd[s_])
            W("t", "sv", he_ready)
            for nt in range(2):
                sl, i6 = pslice()
                evp = ev_ps[i6]
                if isinstance(evp, tuple):
                    W("t", evp[0], evp[1])
                else:
                    W("t", "sv", evp)
                for ic in range(NIC):
                    tm = O("t", lambda e, s_=s_, ic=ic, nt=nt, sl=sl: e.matmul(
                        sl, WD[s_][:, ic * 128:(ic + 1) * 128],
                        hebuf[:, ic * 1024 + nt * 512: ic * 1024 + (nt + 1) * 512],
                        start=(ic == 0), stop=(ic == NIC - 1)), "st")
                fin_wd[s_] = tm
                W("v", "st", tm)
                ev = O("v", lambda e, sl=sl, mc=mc, nt=nt: e.scalar_tensor_tensor(
                    acc[:, mc * 1024 + nt * 512: mc * 1024 + (nt + 1) * 512], sl,
                    msl(mgbc)[:, mc:mc + 1],
                    acc[:, mc * 1024 + nt * 512: mc * 1024 + (nt + 1) * 512],
                    ALU.mult, ALU.add), "sv")
                ev_ps[i6] = ev
                last_ev = ev
        return last_ev

    p.bar("t")
    W("g", "sv", p.cnt["sv"])
    W("g", "dx", dx_x1_all)
    W("g", "st", fin_wo[0])
    W("g", "st", fin_wo[1])
    W("t", "sv", p.cnt["sv"])
    W("t", "ss", p.cnt["ss"])
    load_gu(0, sgw[0], suw[0], fin_wo[0])
    he_last = None
    for ic in range(NIC):
        pair = ic % 2
        if ic + 1 < NIC:
            load_gu((ic + 1) % 2, sgw[ic + 1], suw[ic + 1], fin_gu[(ic + 1) % 2])
        he_last = gate_up_act(-1, ic, pair, None)
    down_proj(lambda mc: sdw[mc], he_last, "sd")

    if ckpt("shared", [("accS", acc[:]), ("he", hebuf[:])]):
        return nc, taps

    v_cb = None
    for e_i in range(NE):
        W("g", "sv", v_cr)
        W("g", "st", p.cnt["st"] if e_i > 0 else 0)
        s_cb = O("g", lambda e, e_i=e_i: e.dma_start(
            out=zrow[0:1, 0:1024], in_=crow[e_i:e_i + 1, :]), "dw")
        W("t", "dw", s_cb)
        W("t", "sv", v_cb)
        for nt in range(2):
            tm = O("t", lambda e, nt=nt: e.matmul(
                pZ[:, nt * 512:(nt + 1) * 512], onesr[:], zrow[0:1, nt * 512:(nt + 1) * 512],
                start=True, stop=True), "st")
            W("v", "st", tm)
            v_cb = O("v", lambda e, nt=nt: e.tensor_copy(
                cbcst[:, nt * 512:(nt + 1) * 512], pZ[:, nt * 512:(nt + 1) * 512]), "sv")
        p._he2[1] = max(p._he2[1] or 0, v_cb) or v_cb
        W("v", "sv", v_cb)
        load_gu(0, wgw[e_i, 0], wuw[e_i, 0], fin_gu[0])
        he_last = None
        for ic in range(NIC):
            pair = ic % 2
            if ic + 1 < NIC:
                load_gu((ic + 1) % 2, wgw[e_i, ic + 1], wuw[e_i, ic + 1], fin_gu[(ic + 1) % 2])
            elif e_i + 1 < NE:
                load_gu((ic + 1) % 2, wgw[e_i + 1, 0], wuw[e_i + 1, 0], fin_gu[(ic + 1) % 2])
            he_last = gate_up_act(e_i, ic, pair, True)
        last_ev = down_proj(lambda mc, e_i=e_i: wdw[e_i, mc], he_last, "wd%d" % e_i)
        if e_i == 0 and ckpt("exp0", [("accE", acc[:]), ("heE", hebuf[:]), ("cbc0", cbcst[:])]):
            return nc, taps
    p.flush()

    W("y", "sv", last_ev)
    dx_out = None
    for mc in range(NM):
        dx_out = O("y", lambda e, mc=mc: e.dma_start(
            out=out[mc * 128:(mc + 1) * 128, :], in_=acc[:, mc * 1024:(mc + 1) * 1024]), "dx")
    W("y", "dx", dx_out)
    p.flush()

    esC.close()
    es.close()
    return nc, dbg



def _tile_w(WT, ncol):
    nin = WT.shape[0] // 128
    o = WT.shape[1]
    return np.ascontiguousarray(
        WT.reshape(nin, 128, o // ncol, ncol).transpose(2, 1, 0, 3)).astype(np.float32)


def prep_in_maps(inp):
    f = lambda k: np.asarray(inp[k], dtype=np.float32)
    x = f("L0_attn_hidden_states")
    Wq, Wk, Wv, Wo = f("L0_attn_Wq"), f("L0_attn_Wk"), f("L0_attn_Wv"), f("L0_attn_Wo")
    gamma, beta = f("L0_attn_gamma"), f("L0_attn_beta")
    rwm = f("L0_ffn_router_w")
    sgm, sum_, sdm = f("L0_ffn_shared_gate"), f("L0_ffn_shared_up"), f("L0_ffn_shared_down")
    wgm, wum, wdm = f("L0_ffn_w_gate"), f("L0_ffn_w_up"), f("L0_ffn_w_down")
    cm, wmodm = f("L0_c"), f("L0_wmod")

    shared = dict(
        wq=_tile_w(Wq.T, 512), wk=_tile_w(Wk.T, 512), wv=_tile_w(Wv.T, 512),
        wo=_tile_w(Wo.T, 128), wmod=_tile_w(wmodm, 512),
        sgw=_tile_w(sgm.T, 128), suw=_tile_w(sum_.T, 128), sdw=_tile_w(sdm.T, 128),
        wgw=np.stack([_tile_w(wgm[e].T, 128) for e in range(NE)]),
        wuw=np.stack([_tile_w(wum[e].T, 128) for e in range(NE)]),
        wdw=np.stack([_tile_w(wdm[e].T, 128) for e in range(NE)]),
        rww=np.ascontiguousarray(rwm.T.reshape(16, 128, 8).transpose(1, 0, 2)),
        gamma4=np.ascontiguousarray(np.broadcast_to(np.tile(gamma, 4)[None, :], (128, 512))),
        beta4=np.ascontiguousarray(np.broadcast_to(np.tile(beta, 4)[None, :], (128, 512))),
        ident=np.eye(128, dtype=np.float32),
    )

    tri = np.triu(np.ones((128, 128), np.float32))
    ones_t = np.ones((128, 128), np.float32)
    zeros_t = np.zeros((128, 128), np.float32)

    in_maps = []
    for core in range(8):
        b, half = core // 2, core % 2
        xr = np.ascontiguousarray(x[b].T).reshape(D, 16, 128)
        perm = [j ^ half for j in range(16)]
        xT_c = np.ascontiguousarray(xr[:, perm, :].reshape(D, TB))
        xown_c = np.ascontiguousarray(xr[:, half::2, :].reshape(D, TOK))
        fm = np.stack([tri if m % 2 == 0 else (zeros_t if half == 0 else ones_t)
                       for m in range(NKT)])
        cbc_c = np.ascontiguousarray(cm[b].reshape(16, 128).T)
        m = dict(shared)
        m.update(xT=xT_c, xown=xown_c, fmask=np.ascontiguousarray(fm), cbc=cbc_c)
        in_maps.append(m)
    return in_maps


def kernel(**inp):
    in_maps = prep_in_maps(inp)
    nc, _ = build_program()
    res = run_bass_kernel_spmd(nc, in_maps, core_ids=list(range(8)))

    full = np.empty((4, TB, D), np.float32)
    for core in range(8):
        b, half = core // 2, core % 2
        o = res.results[core]["out"]
        full[b].reshape(16, 128, D)[half::2] = o.T.reshape(NP, 128, D)
    return full


# version 05
# speedup vs baseline: 1.0486x; 1.0486x over previous
import numpy as np
from contextlib import ExitStack

import concourse.bass as bass
import concourse.mybir as mybir
from concourse.bass_utils import run_bass_kernel_spmd

F32 = mybir.dt.float32
BF16 = mybir.dt.bfloat16
AF = mybir.ActivationFunctionType
ALU = mybir.AluOpType
AX = mybir.AxisListType

D = 2048
TB = 2048
TOK = 1024
NJ = 16
NP = 8
NKT = 16
H = 16
HD = 128
NG = 4
FI = 1408
NIC = 11
NE = 8
NM = 16
EPS = 1e-5

ENG = {"t": "tensor", "v": "vector", "s": "scalar", "g": "gpsimd", "y": "sync"}
DBG = {}
STOP = None
MMAX = 16


class Pr:
    def __init__(self, nc, block, sems):
        self.nc, self.block, self.sems = nc, block, sems
        self.pend = {k: [] for k in ENG}
        self.cnt = {k: 0 for k in sems}
        self.lastw = {}

    def op(self, eng, fn, inc=None):
        amt = 0
        if inc is not None:
            amt = 16 if inc in ("dw", "dx") else 1
            self.cnt[inc] += amt
        sems = self.sems
        dr = eng == "v"

        def run(e, fn=fn, inc=inc, amt=amt, dr=dr):
            if dr:
                e.drain()
            r = fn(e)
            if inc is not None:
                r.then_inc(sems[inc], amt)

        self.pend[eng].append(run)
        return self.cnt[inc] if inc is not None else None

    def wait(self, eng, sem, val):
        if val is None or val <= 0:
            return
        key = (eng, sem)
        if self.lastw.get(key, 0) >= val:
            return
        self.lastw[key] = val
        sems = self.sems
        self.pend[eng].append(lambda e, sem=sem, val=val: e.wait_ge(sems[sem], val))

    def bar(self, eng, srcs=("sv", "ss")):
        for s_ in srcs:
            self.wait(eng, s_, self.cnt[s_])

    def flush(self):
        for eng, lst in self.pend.items():
            if not lst:
                continue

            def mk(lst):
                def f(e):
                    for fn in lst:
                        fn(e)
                return f

            getattr(self.block, ENG[eng])(mk(lst))
            self.pend[eng] = []


def build_program():
    nc = bass.Bass("TRN2", target_bir_lowering=False)
    dram = lambda n, s, ty=F32, k="ExternalInput": nc.dram_tensor(n, s, ty, kind=k).ap()

    xT = dram("xT", [D, TB])
    xown = dram("xown", [D, TOK])
    wq = dram("wq", [NG, 128, NJ, 512])
    wk = dram("wk", [NG, 128, NJ, 512])
    wv = dram("wv", [NG, 128, NJ, 512])
    wo = dram("wo", [NM, 128, NJ, 128])
    wmod = dram("wmod", [24, 128, NJ, 512])
    cbc = dram("cbc", [128, NJ])
    sgw = dram("sgw", [NIC, 128, NJ, 128])
    suw = dram("suw", [NIC, 128, NJ, 128])
    sdw = dram("sdw", [NM, 128, NIC, 128])
    wgw = dram("wgw", [NE, NIC, 128, NJ, 128])
    wuw = dram("wuw", [NE, NIC, 128, NJ, 128])
    wdw = dram("wdw", [NE, NM, 128, NIC, 128])
    rww = dram("rww", [128, NJ, 8])
    gamma4 = dram("gamma4", [128, 512])
    beta4 = dram("beta4", [128, 512])
    fmask = dram("fmask", [2, 128, 128])
    ident = dram("ident", [128, 128])
    out = dram("out", [D, TOK], F32, "ExternalOutput")
    x1s = dram("x1s", [D, TOK], F32, "Internal")
    dbg = {k: dram("dbg_" + k, list(s), F32, "ExternalOutput") for k, s in DBG.items()}

    es = ExitStack()
    sb = lambda n, s, ty: es.enter_context(nc.sbuf_tensor(n, s, ty))
    psal = lambda n: es.enter_context(nc.psum_tensor(n, [128, 1024], F32))

    gam4 = sb("gam4", [128, 512], BF16)
    bet4 = sb("bet4", [128, 512], F32)
    fmk = sb("fmk", [128, 2 * 128], BF16)
    idb = sb("idb", [128, 128], BF16)
    idf = sb("idf", [128, 128], F32)
    onesc = sb("onesc", [128, 1], BF16)
    onesr = sb("onesr", [1, 128], BF16)
    modc = sb("modc", [128, 96], F32)
    cbs = sb("cbs", [128, NJ], F32)
    silc = sb("silc", [128, NJ], BF16)
    mder = sb("mder", [128, 96], F32)
    mtmp = sb("mtmp", [128, 48], F32)
    rowA = sb("rowA", [32, 1024], F32)
    rowB = sb("rowB", [32, 1024], F32)
    fmst = sb("fmst", [128, 384], F32)
    junk = sb("junk", [128, 128], BF16)
    arow = sb("arow", [1, 512], BF16)
    brow = sb("brow", [1, 512], BF16)
    zrow = sb("zrow", [1, 1024], BF16)
    mrow = sb("mrow", [1, 1024], BF16)
    wmb = sb("wmb", [128, 1024], BF16)
    rwsb = sb("rwsb", [128, NJ * 8], BF16)
    wb = sb("wb", [128, 16384], BF16)
    wbf = wb[:].bitcast(F32)

    pO = psal("pO")
    pS0 = psal("pS0")
    pS1 = psal("pS1")
    pZ = psal("pZ")

    esH = ExitStack()
    h1 = esH.enter_context(nc.sbuf_tensor("h1", [128, NJ * TB], BF16))
    esA = ExitStack()
    sbA = lambda n, s, ty: esA.enter_context(nc.sbuf_tensor(n, s, ty))
    xbf = sbA("xbf", [128, NJ * TB], BF16)
    sct = sbA("sct", [128, 1024], BF16)
    abc1 = sbA("abc1", [128, TB], BF16)
    bbc1 = sbA("bbc1", [128, TB], BF16)

    sems = {k: es.enter_context(nc.semaphore(k)) for k in ("st", "sv", "ss", "dw", "dx")}
    block = es.enter_context(nc.Block())
    p = Pr(nc, block, sems)
    p._sig = None
    p._he = None
    p._trev = [None, None, None]
    p._sig2 = [None, None]
    p._he2 = [None, None]
    W, O = p.wait, p.op

    taps = {}

    def ckpt(name, tap_list):
        if STOP != name:
            return False
        for tag, ap in tap_list:
            t = nc.dram_tensor("tap_" + tag, [ap.shape[0], ap.shape[1]], ap.dtype,
                               kind="ExternalOutput").ap()
            p.bar("y", ("sv", "ss", "st"))
            O("y", lambda e, t=t, ap=ap: e.dma_start(out=t, in_=ap), "dx")
            taps[tag] = t
        W("y", "dx", p.cnt["dx"])
        p.flush()
        return True

    SA, CA, GA, SBv, CB, GB = range(6)
    mg1, mg2, mgac, mgbc, mgbg2, mgbs2 = range(6)
    msl = lambda i: mder[:, i * 16:(i + 1) * 16]

    c_bet4 = O("y", lambda e: e.dma_start(out=bet4[:], in_=beta4[:]), "dx")
    c_idf = O("y", lambda e: e.dma_start(out=idf[:], in_=ident[:]), "dx")
    c_cbs = O("y", lambda e: e.dma_start(out=cbs[:], in_=cbc[:]), "dx")
    c_gam = O("g", lambda e: e.dma_start(out=gam4[:], in_=gamma4[:]), "dw")
    c_idb = O("g", lambda e: e.dma_start(out=idb[:], in_=ident[:]), "dw")
    c_rw = O("g", lambda e: e.dma_start(out=rwsb[:], in_=rww[:].rearrange("p j e -> p (j e)")), "dw")
    c_fmk = None
    for m in range(2):
        c_fmk = O("g", lambda e, m=m: e.dma_start(out=fmk[:, m * 128:(m + 1) * 128], in_=fmask[m]), "dw")
    O("v", lambda e: e.memset(onesc[:], 1.0))
    O("v", lambda e: e.memset(onesr[:], 1.0))
    O("v", lambda e: e.memset(rowA[:], 0.0))
    O("v", lambda e: e.memset(rowB[:], 0.0), "sv")
    W("s", "dx", c_cbs)
    s_sil = O("s", lambda e: e.activation(silc[:], cbs[:], AF.Silu), "ss")

    dw_wm = [None] * 24
    dw_x = [None] * NJ
    wm_fin = [None] * 24

    def dma_wm(nb):
        slot = wb[:, (nb % 2) * 8192:(nb % 2) * 8192 + 8192]
        if nb >= 2:
            W("g", "st", wm_fin[nb - 2])
        dw_wm[nb] = O("g", lambda e, nb=nb, slot=slot: e.dma_start(
            out=slot, in_=wmod[nb].rearrange("p j c -> p (j c)")), "dw")

    dma_wm(0)
    dma_wm(1)
    for i in range(NJ):
        dw_x[i] = O("g", lambda e, i=i: e.dma_start(
            out=xbf[:, i * TB:(i + 1) * TB], in_=xT[i * 128:(i + 1) * 128, :]), "dw")

    xsq = sct
    mod_evac = [None] * 24
    col_fin = [None] * 24
    st_stats = None
    s_xsq = None

    def emit_mod_block(nb):
        slot = wb[:, (nb % 2) * 8192:(nb % 2) * 8192 + 8192]
        bank = pO if nb % 2 == 0 else pS0
        prow = bank[0:1, 0:512]
        W("t", "dw", dw_wm[nb])
        W("t", "ss", s_sil)
        if nb >= 2:
            W("t", "sv", mod_evac[nb - 2])
        for jc in range(NJ):
            wm_fin[nb] = O("t", lambda e, jc=jc, slot=slot, prow=prow: e.matmul(
                prow, silc[:, jc:jc + 1], slot[:, jc * 512:(jc + 1) * 512],
                start=(jc == 0), stop=(jc == NJ - 1)), "st")
        W("v", "st", wm_fin[nb])
        if nb >= 2:
            W("v", "st", col_fin[nb - 2])
        mod_evac[nb] = O("v", lambda e, nb=nb, prow=prow: e.tensor_copy(
            mrow[0:1, (nb % 2) * 512:(nb % 2) * 512 + 512], prow), "sv")
        W("t", "sv", mod_evac[nb])
        v_mcol = None
        for jcl in range(4):
            col = (nb // 4) * 16 + (nb % 4) * 4 + jcl
            pcol = bank[:, 520 + jcl:521 + jcl]
            if nb >= 2:
                W("t", "sv", mod_evac[nb - 2])
            col_fin[nb] = O("t", lambda e, nb=nb, jcl=jcl, pcol=pcol: e.matmul(
                pcol, mrow[0:1, (nb % 2) * 512 + jcl * 128:(nb % 2) * 512 + jcl * 128 + 128],
                onesr[0:1, 0:1], start=True, stop=True), "st")
            W("v", "st", col_fin[nb])
            v_mcol = O("v", lambda e, col=col, pcol=pcol: e.tensor_copy(
                modc[:, col:col + 1], pcol), "sv")
        mod_evac[nb] = v_mcol
        if nb + 2 < 24:
            dma_wm(nb + 2)

    for i in range(NJ):
        emit_mod_block(i)
        if i < 8:
            emit_mod_block(i + 16) if False else None
        jc = i
        for hf in range(2):
            W("s", "dw", dw_x[jc])
            if s_xsq is not None:
                W("s", "st", st_stats)
            s_xsq = O("s", lambda e, jc=jc, hf=hf: e.activation(
                xsq[:], xbf[:, jc * TB + hf * 1024: jc * TB + (hf + 1) * 1024],
                AF.Square), "ss")
            W("t", "ss", s_xsq)
            W("t", "dw", dw_x[jc])
            for nt2 in range(2):
                r0 = 32 * nt2
                c0 = hf * 512
                off = jc * TB + hf * 1024 + nt2 * 512
                O("t", lambda e, off=off, r0=r0, c0=c0, jc=jc: e.matmul(
                    pZ[r0:r0 + 1, c0:c0 + 512], onesc[:], xbf[:, off:off + 512],
                    start=(jc == 0), stop=(jc == NJ - 1)), "st")
                st_stats = O("t", lambda e, nt2=nt2, r0=r0, c0=c0, jc=jc: e.matmul(
                    pS1[r0:r0 + 1, c0:c0 + 512], onesc[:], xsq[:, nt2 * 512:(nt2 + 1) * 512],
                    start=(jc == 0), stop=(jc == NJ - 1)), "st")

    for nb in range(NJ, 24):
        emit_mod_block(nb)

    def emit_clamp(src_i, dst):
        g_ = modc[:, src_i * 16:(src_i + 1) * 16]
        O("v", lambda e, g_=g_: e.tensor_scalar(mtmp[:, 0:16], g_, 0.0, None, ALU.is_ge))
        O("v", lambda e, g_=g_: e.tensor_scalar(mtmp[:, 16:32], g_, 0.25, None, ALU.max))
        O("v", lambda e, g_=g_: e.tensor_scalar(mtmp[:, 32:48], g_, -0.25, None, ALU.min))
        O("v", lambda e: e.tensor_tensor(mtmp[:, 16:32], mtmp[:, 16:32], mtmp[:, 32:48], ALU.subtract))
        O("v", lambda e: e.tensor_tensor(mtmp[:, 16:32], mtmp[:, 16:32], mtmp[:, 0:16], ALU.mult))
        return O("v", lambda e, dst=dst: e.tensor_tensor(msl(dst), mtmp[:, 16:32], mtmp[:, 32:48], ALU.add), "sv")

    W("v", "sv", mod_evac[7])
    v_mder = O("v", lambda e: e.tensor_scalar(msl(mg1), modc[:, CA * 16:(CA + 1) * 16], 1.0, None, ALU.add), "sv")

    def row_stats_to_ab(n_tok, st_after, sumap, sqap, adst, bdst, pbank):
        W("v", "st", st_after)
        sa = sb_ = None
        last_bc = [None, None]
        last_v = None
        for k in range(n_tok // 512):
            W("v", "ss", sb_)
            O("v", lambda e, k=k: e.tensor_copy(rowA[0:1, 0:512], sumap(k)))
            O("v", lambda e, k=k: e.tensor_copy(rowA[0:1, 512:1024], sqap(k)))
            O("v", lambda e: e.transpose(rowB[:, 0:1024], rowA[:, 0:1024]))
            ss_ = lambda c: rowB[:, c:512:32]
            qs_ = lambda c: rowB[:, 512 + c:1024:32]
            iD = 1.0 / D
            O("v", lambda e, ss_=ss_, iD=iD: e.tensor_scalar(ss_(2), ss_(0), iD, None, ALU.mult))
            O("v", lambda e, ss_=ss_, qs_=qs_, iD=iD: e.tensor_scalar(ss_(3), qs_(0), iD, None, ALU.mult))
            O("v", lambda e, ss_=ss_: e.tensor_tensor(ss_(4), ss_(2), ss_(2), ALU.mult))
            O("v", lambda e, ss_=ss_: e.tensor_tensor(ss_(5), ss_(3), ss_(4), ALU.subtract))
            O("v", lambda e, ss_=ss_: e.tensor_scalar(ss_(5), ss_(5), EPS, None, ALU.add))
            O("v", lambda e, ss_=ss_: e.reciprocal(ss_(6), ss_(5)))
            vs = O("v", lambda e: e.drain(), "sv")
            W("s", "sv", vs)
            s1 = O("s", lambda e, ss_=ss_: e.activation(ss_(7), ss_(6), AF.Sqrt), "ss")
            W("v", "ss", s1)
            O("v", lambda e, ss_=ss_, qs_=qs_: e.tensor_tensor(qs_(0), ss_(2), ss_(7), ALU.mult))
            O("v", lambda e, qs_=qs_: e.tensor_scalar(qs_(0), qs_(0), -1.0, None, ALU.mult))
            O("v", lambda e, ss_=ss_: e.tensor_scalar(ss_(0), ss_(7), 1.0, None, ALU.mult))
            v2 = O("v", lambda e: e.transpose(rowA[:, 0:1024], rowB[:, 0:1024]), "sv")
            W("s", "sv", v2)
            W("s", "st", last_bc[0])
            sa = O("s", lambda e: e.copy(arow[0:1, 0:512], rowA[0:1, 0:512]), "ss")
            sb_ = O("s", lambda e: e.copy(brow[0:1, 0:512], rowA[0:1, 512:1024]), "ss")
            W("t", "ss", sb_)
            for which, row_t, dst in ((0, arow, adst), (1, brow, bdst)):
                col = pbank[:, (which % 2) * 512:(which % 2) * 512 + 512]
                W("t", "sv", last_bc[1] if False else None)
                tm = O("t", lambda e, col=col, row_t=row_t: e.matmul(
                    col, onesr[:], row_t[0:1, 0:512], start=True, stop=True), "st")
                last_bc[0] = tm
                W("v", "st", tm)
                last_v = O("v", lambda e, col=col, dst=dst, k=k: e.tensor_copy(
                    dst[:, k * 512:(k + 1) * 512], col), "sv")
            W("t", "sv", last_v)
        return last_v

    W("t", "sv", mod_evac[7])
    vbb1 = row_stats_to_ab(
        TB, st_stats,
        lambda k: pZ[32 * (k % 2):32 * (k % 2) + 1, (k // 2) * 512:(k // 2) * 512 + 512],
        lambda k: pS1[32 * (k % 2):32 * (k % 2) + 1, (k // 2) * 512:(k // 2) * 512 + 512],
        abc1, bbc1, pO)

    W("v", "sv", vbb1)
    W("v", "sv", v_mder)
    h1_done = None
    for jc in range(NJ):
        W("v", "dw", dw_x[jc])
        for hf in range(2):
            off = jc * TB + hf * 1024
            O("v", lambda e, off=off, hf=hf: e.tensor_tensor(
                sct[:], xbf[:, off:off + 1024], abc1[:, hf * 1024:(hf + 1) * 1024], ALU.mult))
            O("v", lambda e, hf=hf: e.tensor_tensor(
                sct[:], sct[:], bbc1[:, hf * 1024:(hf + 1) * 1024], ALU.add))
            h1_done = O("v", lambda e, off=off, jc=jc: e.tensor_scalar(
                h1[:, off:off + 1024], sct[:],
                msl(mg1)[:, jc:jc + 1], modc[:, SA * 16 + jc:SA * 16 + jc + 1],
                ALU.mult, ALU.add), "sv")
    if ckpt("h1", [("h1", h1[:]), ("abc1", abc1[:]), ("bbc1", bbc1[:])]):
        return nc, taps
    p.flush()

    esA.close()
    esB = ExitStack()
    sbB = lambda n, s, ty: esB.enter_context(nc.sbuf_tensor(n, s, ty))
    osb = sbB("osb", [128, H * 1024], BF16)
    qg = sbB("qg", [128, NP * 512], BF16)
    kg = sbB("kg", [128, NKT * 512], BF16)
    vg = sbB("vg", [128, NKT * 512], BF16)
    ssq = sbB("ssq", [128, 2 * 1024], BF16)
    fqT = sbB("fqT", [128, 1024], BF16)
    fkT = sbB("fkT", [128, 2048], BF16)
    rz = sbB("rz", [128, 1024], BF16)

    QLO = [128 * (m // 2) for m in range(NKT)]
    wslot = [wb[:, 0:8192], wb[:, 8192:16384]]
    dw_w = [None, None]
    fin_w = [wm_fin[22], wm_fin[23]]
    slot_i = [0]

    def load_w(src_ap, scale_gamma):
        s_ = slot_i[0] % 2
        slot_i[0] += 1
        W("g", "st", fin_w[s_])
        dw_w[s_] = O("g", lambda e, s_=s_, src_ap=src_ap: e.dma_start(out=wslot[s_], in_=src_ap), "dw")
        vfin = None
        if scale_gamma:
            W("v", "dw", dw_w[s_])
            W("v", "dw", c_gam)
            for jc in range(NJ):
                vfin = O("v", lambda e, s_=s_, jc=jc: e.tensor_tensor(
                    wslot[s_][:, jc * 512:(jc + 1) * 512], wslot[s_][:, jc * 512:(jc + 1) * 512],
                    gam4[:], ALU.mult), "sv")
        return s_, vfin

    sQ, vQ = load_w(wq[0].rearrange("p j c -> p (j c)"), True)
    PSL = [(pO, 0), (pO, 512), (pS0, 0), (pS0, 512), (pS1, 0), (pS1, 512)]
    pi = [0]
    ev_ps = [vbb1, vbb1, None, None, None, None]

    def pslice():
        i = pi[0] % 6
        pi[0] += 1
        bank, off = PSL[i]
        return bank[:, off:off + 512], i

    st_group_end = None
    last_head_v = None
    iHD = 1.0 / HD
    MU, RS = 192, 288

    def fm_batch(c0, tiles, ss_after):
        sl_ = lambda b: fmst[:, b + c0:b + c0 + 16]
        W("v", "ss", ss_after)
        O("v", lambda e, sl_=sl_: e.tensor_scalar(sl_(MU), sl_(0), iHD, None, ALU.mult))
        O("v", lambda e, sl_=sl_: e.tensor_scalar(sl_(RS), sl_(96), iHD, None, ALU.mult))
        O("v", lambda e, sl_=sl_: e.tensor_tensor(sl_(0), sl_(MU), sl_(MU), ALU.mult))
        O("v", lambda e, sl_=sl_: e.tensor_tensor(sl_(RS), sl_(RS), sl_(0), ALU.subtract))
        O("v", lambda e, sl_=sl_: e.tensor_scalar(sl_(RS), sl_(RS), EPS, None, ALU.add))
        O("v", lambda e, sl_=sl_: e.reciprocal(sl_(96), sl_(RS)))
        vrc = O("v", lambda e: e.nop(), "sv")
        W("s", "sv", vrc)
        srs = O("s", lambda e, sl_=sl_: e.activation(sl_(RS), sl_(96), AF.Sqrt), "ss")
        W("v", "ss", srs)
        if c0 < 32:
            O("v", lambda e, sl_=sl_: e.tensor_scalar(sl_(RS), sl_(RS), HD ** -0.5, None, ALU.mult))
        vfm = None
        for dst, base, soff in tiles:
            for hh in range(4):
                vfm = O("v", lambda e, dst=dst, base=base, soff=soff, hh=hh: e.tensor_scalar(
                    dst[:, base + hh * 128: base + hh * 128 + 128],
                    dst[:, base + hh * 128: base + hh * 128 + 128],
                    fmst[:, MU + soff + hh:MU + soff + hh + 1],
                    fmst[:, RS + soff + hh:RS + soff + hh + 1],
                    ALU.subtract, ALU.mult), "sv")
        return vfm


    for g in range(NG):
        vfm = None
        sK = vK = sV = vV = None
        sK, vK = load_w(wk[g].rearrange("p j c -> p (j c)"), True)
        for kind in ("q", "k", "v"):
            if kind == "q":
                s_, vfin, ntile, dst, coloff = sQ, vQ, NP, qg, 256
            elif kind == "k":
                s_, vfin, ntile, dst, coloff = sK, vK, NKT, kg, 128
            else:
                s_, vfin, ntile, dst, coloff = sV, vV, NKT, vg, 128
            W("t", "dw", dw_w[s_])
            W("t", "sv", vfin)
            W("t", "sv", h1_done)
            for t_ in range(ntile):
                sl, i6 = pslice()
                evp = ev_ps[i6]
                if isinstance(evp, tuple):
                    W("t", evp[0], evp[1])
                else:
                    W("t", "sv", evp)
                W("t", "ss", p.cnt["ss"] if g > 0 and t_ == 0 and kind == "q" else None)
                for jc in range(NJ):
                    tm = O("t", lambda e, t_=t_, jc=jc, sl=sl, s_=s_, coloff=coloff: e.matmul(
                        sl, h1[:, jc * TB + coloff * t_: jc * TB + coloff * t_ + 128],
                        wslot[s_][:, jc * 512:(jc + 1) * 512],
                        start=(jc == 0), stop=(jc == NJ - 1)), "st")
                fin_w[s_] = tm
                W("v", "st", tm)
                W("v", "st", st_group_end)
                if kind == "v":
                    W("s", "st", tm)
                    W("s", "st", st_group_end)
                    ev = O("s", lambda e, sl=sl, t_=t_: e.activation(
                        vg[:, t_ * 512:(t_ + 1) * 512], sl, AF.Copy), "ss")
                else:
                    W("v", "dx", c_bet4)
                    ev = O("v", lambda e, sl=sl, dst=dst, t_=t_: e.tensor_tensor(
                        dst[:, t_ * 512:(t_ + 1) * 512], sl, bet4[:], ALU.add), "sv")
                ev_ps[i6] = ("ss", ev) if kind == "v" else ev
                if kind != "v":
                    soff = (t_ * 4) if kind == "q" else (32 + t_ * 4)
                    O("v", lambda e, dst=dst, t_=t_, soff=soff: e.tensor_reduce(
                        fmst[:, soff:soff + 4],
                        dst[:, t_ * 512:(t_ + 1) * 512].rearrange("p (h d) -> p h d", d=128),
                        AX.X, ALU.add))
                    W("s", "sv", ev)
                    s_sq = None
                    for hh in range(4):
                        s_sq = O("s", lambda e, dst=dst, t_=t_, hh=hh, soff=soff: e.activation(
                            junk[:, 0:128],
                            dst[:, t_ * 512 + hh * 128: t_ * 512 + hh * 128 + 128],
                            AF.Square, accum_out=fmst[:, 96 + soff + hh:96 + soff + hh + 1]),
                            "ss" if hh == 3 else None)
                    if t_ % 4 == 3:
                        c0 = soff - 12
                        tiles = [(dst, (t_ - 3 + j) * 512, c0 + 4 * j) for j in range(4)]
                        vfm = fm_batch(c0, tiles, s_sq)
            if kind == "q":
                sV, vV = load_w(wv[g].rearrange("p j c -> p (j c)"), False)
            elif kind == "k" and g < NG - 1:
                sQ, vQ = load_w(wq[g + 1].rearrange("p j c -> p (j c)"), True)

        sqm = [None, None]
        avf = [None, None]
        st_scores_last = None
        if g == 0 and ckpt("fm0", [("qg", qg[:]), ("kg", kg[:]), ("vg", vg[:]),
                                   ("fmst", fmst[:])]):
            return nc, taps
        W("t", "ss", p.cnt["ss"])
        for hh in range(4):
            h_ = g * 4 + hh
            W("t", "sv", vfm)
            W("t", "dw", c_idb)
            W("t", "ss", sqm[0])
            W("t", "ss", sqm[1])
            vtr = None
            for grp8 in range(3):
                bank, boff = ((pS0, 0), (pS1, 0), (pS0, 1024))[grp8]
                W("t", "sv", p._trev[grp8])
                ttr = None
                for j in range(8):
                    t_ = grp8 * 8 + j
                    src, base = (qg, t_ * 512) if t_ < NP else (kg, (t_ - NP) * 512)
                    sl = bank[:].bitcast(BF16)[:, boff + j * 128:boff + j * 128 + 128]
                    ttr = O("t", lambda e, src=src, base=base, hh=hh, sl=sl: e.transpose(
                        sl, src[:, base + hh * 128: base + hh * 128 + 128], idb[:]), "st")
                W("v", "st", ttr)
                W("v", "st", st_scores_last)
                dst2, doff = ((fqT, 0), (fkT, 0), (fkT, 1024))[grp8]
                full = bank[:].bitcast(BF16)[:, boff:boff + 1024]
                vtr = p._trev[grp8] = O("v", lambda e, full=full, dst2=dst2, doff=doff: e.tensor_copy(
                    dst2[:, doff:doff + 1024], full), "sv")
            if g == 0 and hh == 0 and ckpt("tr0", [("fqT", fqT[:]), ("fkT", fkT[:])]):
                return nc, taps
            for m in range(MMAX):
                slot = m % 2
                pS = (pS0, pS1)[slot]
                lo = QLO[m]
                W("t", "sv", vtr)
                W("t", "ss", sqm[slot])
                blocks = [(lo, 512), (512, 1024)] if lo < 512 else [(lo, 1024)]
                tb_ = None
                for (b0, b1) in blocks:
                    tb_ = O("t", lambda e, pS=pS, b0=b0, b1=b1, m=m: e.matmul(
                        pS[:, b0:b1], fkT[:, m * 128:(m + 1) * 128], fqT[:, b0:b1],
                        start=True, stop=True), "st")
                st_scores_last = tb_
                W("s", "st", tb_)
                W("s", "st", avf[slot])
                sqm[slot] = O("s", lambda e, pS=pS, lo=lo, slot=slot: e.activation(
                    ssq[:, slot * 1024 + lo: slot * 1024 + 1024], pS[:, lo:1024], AF.Square), "ss")
                pf = m // 2
                W("v", "ss", sqm[slot])
                W("v", "dw", c_fmk)
                vmk = O("v", lambda e, slot=slot, pf=pf, m=m: e.tensor_tensor(
                    ssq[:, slot * 1024 + pf * 128: slot * 1024 + pf * 128 + 128],
                    ssq[:, slot * 1024 + pf * 128: slot * 1024 + pf * 128 + 128],
                    fmk[:, (m % 2) * 128:(m % 2) * 128 + 128], ALU.mult), "sv")
                W("t", "ss", sqm[slot])
                for (b0, b1) in blocks:
                    if b0 <= 128 * pf < b1:
                        W("t", "sv", vmk)
                    O("t", lambda e, m=m, hh=hh, slot=slot, b0=b0, b1=b1: e.matmul(
                        pO[:, b0:b1], vg[:, m * 512 + hh * 128: m * 512 + hh * 128 + 128],
                        ssq[:, slot * 1024 + b0: slot * 1024 + b1],
                        start=(m == 0), stop=(m == (7 if b1 <= 512 else NKT - 1))), "st")
                    avf[slot] = O("t", lambda e, m=m, slot=slot, b0=b0, b1=b1: e.matmul(
                        pZ[0:1, b0:b1], onesc[:],
                        ssq[:, slot * 1024 + b0: slot * 1024 + b1],
                        start=(m == 0), stop=(m == (7 if b1 <= 512 else NKT - 1))), "st")
            if g == 0 and hh == 0 and ckpt("av0", [("ssq", ssq[:])]):
                return nc, taps
            W("v", "st", avf[0])
            W("v", "st", avf[1])
            O("v", lambda e: e.tensor_copy(rowA[0:1, 0:512], pZ[0:1, 0:512]))
            O("v", lambda e: e.tensor_copy(rowA[0:1, 512:1024], pZ[0:1, 512:1024]))
            O("v", lambda e: e.drain())
            O("v", lambda e: e.transpose(rowB[:, 0:1024], rowA[:, 0:1024]))
            zsl = lambda c: rowB[:, c:1024:32]
            O("v", lambda e, zsl=zsl: e.tensor_scalar(zsl(1), zsl(0), EPS, None, ALU.add))
            O("v", lambda e, zsl=zsl: e.reciprocal(zsl(0), zsl(1)))
            O("v", lambda e: e.drain())
            O("v", lambda e: e.transpose(rowA[:, 0:1024], rowB[:, 0:1024]))
            vz = O("v", lambda e: e.tensor_copy(zrow[0:1, 0:1024], rowA[0:1, 0:1024]), "sv")
            W("t", "sv", vz)
            W("t", "sv", last_head_v)
            for nt in range(2):
                tm = O("t", lambda e, nt=nt: e.matmul(
                    pZ[:, nt * 512:(nt + 1) * 512], onesr[:], zrow[0:1, nt * 512:(nt + 1) * 512],
                    start=True, stop=True), "st")
                W("v", "st", tm)
                O("v", lambda e, nt=nt: e.tensor_copy(rz[:, nt * 512:(nt + 1) * 512], pZ[:, nt * 512:(nt + 1) * 512]))
            for nt in range(2):
                last_head_v = O("v", lambda e, nt=nt, h_=h_: e.tensor_tensor(
                    osb[:, h_ * 1024 + nt * 512: h_ * 1024 + (nt + 1) * 512],
                    pO[:, nt * 512:(nt + 1) * 512], rz[:, nt * 512:(nt + 1) * 512], ALU.mult), "sv")
            W("t", "sv", last_head_v)
            if g == 0 and hh == 0 and ckpt("head0", [("osb0", osb[:, 0:1024]), ("fqT", fqT[:]), ("fkT", fkT[:]), ("rz", rz[:])]):
                return nc, taps
        st_group_end = p.cnt["st"]
        ev_ps[:] = [last_head_v] * 6
        p.flush()

    W("v", "sv", mod_evac[11])
    v_gac = emit_clamp(GA, mgac)
    woslot = [wb[:, 0:2048], wb[:, 2048:4096]]
    x1slot = [wbf[:, 4096:5120], wbf[:, 5120:6144]]
    dw_wo = [None, None]
    fin_wo = [fin_w[0], fin_w[1]]
    dx_x1 = [None, None]
    v_x1 = [None, None]
    for mc in range(NM):
        s_ = mc % 2
        W("g", "st", fin_wo[s_])
        dw_wo[s_] = O("g", lambda e, mc=mc, s_=s_: e.dma_start(
            out=woslot[s_], in_=wo[mc].rearrange("p j c -> p (j c)")), "dw")
        W("t", "dw", dw_wo[s_])
        W("t", "sv", last_head_v)
        for nt in range(2):
            sl, i6 = pslice()
            evp = ev_ps[i6]
            if isinstance(evp, tuple):
                W("t", evp[0], evp[1])
            else:
                W("t", "sv", evp)
            W("t", "ss", p.cnt["ss"] if mc == 0 and nt == 0 else None)
            for ic in range(NJ):
                tm = O("t", lambda e, s_=s_, ic=ic, nt=nt, sl=sl: e.matmul(
                    sl, woslot[s_][:, ic * 128:(ic + 1) * 128],
                    osb[:, ic * 1024 + nt * 512: ic * 1024 + (nt + 1) * 512],
                    start=(ic == 0), stop=(ic == NJ - 1)), "st")
            fin_wo[s_] = tm
            W("v", "st", tm)
            W("v", "dx", dx_x1[s_])
            W("v", "sv", v_gac)
            ev = O("v", lambda e, sl=sl, s_=s_, nt=nt, mc=mc: e.tensor_scalar(
                x1slot[s_][:, nt * 512:(nt + 1) * 512], sl,
                msl(mgac)[:, mc:mc + 1], None, ALU.mult), "sv")
            ev_ps[i6] = ev
        W("y", "sv", ev)
        dx_x1[s_] = O("y", lambda e, mc=mc, s_=s_: e.dma_start(
            out=x1s[mc * 128:(mc + 1) * 128, :], in_=x1slot[s_][:, 0:1024]), "dx")
    dx_x1_all = p.cnt["dx"]
    if ckpt("oproj", [("osb", osb[:])]):
        return nc, taps
    p.flush()

    esB.close()
    esH.close()
    esC = ExitStack()
    sbC = lambda n, s, ty: esC.enter_context(nc.sbuf_tensor(n, s, ty))
    acc = sbC("acc", [128, NM * 1024], F32)
    h2 = sbC("h2", [128, NJ * 1024], BF16)
    hebuf = sbC("hebuf", [128, NIC * 1024], BF16)
    sgact = sbC("sgact", [128, 1024], F32)
    tf1 = sbC("tf1", [128, 1024], F32)
    abc2 = sbC("abc2", [128, 1024], F32)
    bbc2 = sbC("bbc2", [128, 1024], F32)
    ltok = sbC("ltok", [128, 64], F32)
    etok = sbC("etok", [128, 64], F32)
    ctok = sbC("ctok", [128, 64], F32)
    lgr = rowA[0:8, 0:1024]
    crow = rowB[0:8, 0:1024]
    stok = sbC("stok", [128, 64], F32)
    cbcst = sbC("cbcst", [128, 1024], BF16)

    W("v", "sv", mod_evac[23])
    O("v", lambda e: e.tensor_scalar(msl(mg2), modc[:, CB * 16:(CB + 1) * 16], 1.0, None, ALU.add))
    emit_clamp(GB, mgbc)
    O("v", lambda e: e.tensor_tensor(msl(mgbg2), msl(mgbc), msl(mg2), ALU.mult))
    O("v", lambda e: e.tensor_tensor(msl(mgbs2), msl(mgbc), modc[:, SBv * 16:(SBv + 1) * 16], ALU.mult), "sv")

    xoslot = [wbf[:, 0:1024], wbf[:, 1024:2048]]
    x1ld = [wbf[:, 2048:3072], wbf[:, 3072:4096]]
    acc_bf = hebuf[:, 0:1024]
    acc_sq = hebuf[:, 1024:2048]
    v_acc = [None, None]
    st_l2 = None
    s_l2 = None
    W("y", "dx", dx_x1_all)
    for mc in range(NM):
        s_ = mc % 2
        W("y", "sv", v_acc[s_])
        W("y", "st", fin_wo[s_] if mc < 2 else None)
        O("y", lambda e, mc=mc, s_=s_: e.dma_start(
            out=xoslot[s_][:, 0:1024], in_=xown[mc * 128:(mc + 1) * 128, :]), "dx")
        dxv = O("y", lambda e, mc=mc, s_=s_: e.dma_start(
            out=x1ld[s_][:, 0:1024], in_=x1s[mc * 128:(mc + 1) * 128, :]), "dx")
        W("v", "dx", dxv)
        v_acc[s_] = O("v", lambda e, mc=mc, s_=s_: e.tensor_tensor(
            acc[:, mc * 1024:(mc + 1) * 1024], xoslot[s_][:, 0:1024], x1ld[s_][:, 0:1024], ALU.add), "sv")
        W("s", "sv", v_acc[s_])
        W("s", "st", st_l2)
        O("s", lambda e, mc=mc: e.activation(acc_bf, acc[:, mc * 1024:(mc + 1) * 1024], AF.Copy))
        s_l2 = O("s", lambda e, mc=mc: e.activation(acc_sq, acc[:, mc * 1024:(mc + 1) * 1024], AF.Square), "ss")
        W("t", "ss", s_l2)
        for nt in range(2):
            r0 = 32 * nt
            O("t", lambda e, nt=nt, r0=r0, mc=mc: e.matmul(
                pZ[r0:r0 + 1, 0:512], onesc[:], acc_bf[:, nt * 512:(nt + 1) * 512],
                start=(mc == 0), stop=(mc == NM - 1)), "st")
            st_l2 = O("t", lambda e, nt=nt, r0=r0, mc=mc: e.matmul(
                pZ[r0:r0 + 1, 512:1024], onesc[:], acc_sq[:, nt * 512:(nt + 1) * 512],
                start=(mc == 0), stop=(mc == NM - 1)), "st")

    p.bar("t")
    vbb2 = row_stats_to_ab(
        TOK, st_l2,
        lambda k: pZ[32 * k:32 * k + 1, 0:512],
        lambda k: pZ[32 * k:32 * k + 1, 512:1024],
        abc2, bbc2, pS0)
    ev_ps[2] = ev_ps[3] = vbb2

    if ckpt("ln2r", [("acc", acc[:])]):
        return nc, taps

    h2_ms = [None] * NM
    W("v", "sv", vbb2)
    h2g_last = None
    for mc in range(NM):
        O("v", lambda e, mc=mc: e.tensor_tensor(tf1[:], acc[:, mc * 1024:(mc + 1) * 1024], abc2[:], ALU.mult))
        O("v", lambda e: e.tensor_tensor(tf1[:], tf1[:], bbc2[:], ALU.add))
        O("v", lambda e, mc=mc: e.scalar_tensor_tensor(
            acc[:, mc * 1024:(mc + 1) * 1024], tf1[:], msl(mgbg2)[:, mc:mc + 1],
            acc[:, mc * 1024:(mc + 1) * 1024], ALU.mult, ALU.add))
        O("v", lambda e, mc=mc: e.tensor_scalar(
            acc[:, mc * 1024:(mc + 1) * 1024], acc[:, mc * 1024:(mc + 1) * 1024],
            msl(mgbs2)[:, mc:mc + 1], None, ALU.add))
        h2g_last = h2_ms[mc] = O("v", lambda e, mc=mc: e.tensor_scalar(
            h2[:, mc * 1024:(mc + 1) * 1024], tf1[:],
            msl(mg2)[:, mc:mc + 1], modc[:, SBv * 16 + mc:SBv * 16 + mc + 1],
            ALU.mult, ALU.add), "sv")
    p.flush()

    if ckpt("h2", [("h2", h2[:]), ("acc2", acc[:])]):
        return nc, taps

    W("t", "dw", c_rw)
    st_lg = None
    for jc in range(NJ):
        W("t", "sv", h2_ms[jc])
        for nt in range(2):
            st_lg = O("t", lambda e, nt=nt, jc=jc: e.matmul(
                pZ[0:8, nt * 512:(nt + 1) * 512], rwsb[:, jc * 8:(jc + 1) * 8],
                h2[:, jc * 1024 + nt * 512: jc * 1024 + (nt + 1) * 512],
                start=(jc == 0), stop=(jc == NJ - 1)), "st")
    W("v", "st", st_lg)
    v_lg = None
    for nt in range(2):
        v_lg = O("v", lambda e, nt=nt: e.tensor_copy(
            lgr[:, nt * 512:(nt + 1) * 512], pZ[0:8, nt * 512:(nt + 1) * 512]), "sv")
    W("t", "sv", v_lg)
    W("t", "dx", c_idf)
    v_lt = None
    v_lt_slot = [None, None]
    for tt in range(8):
        sl = pZ[:, 512 * (tt % 2):512 * (tt % 2) + 8]
        W("t", "sv", v_lt_slot[tt % 2])
        ttr = O("t", lambda e, tt=tt, sl=sl: e.transpose(sl, lgr[:, tt * 128:(tt + 1) * 128], idf[0:8, 0:8]), "st")
        W("v", "st", ttr)
        v_lt = v_lt_slot[tt % 2] = O("v", lambda e, tt=tt, sl=sl: e.tensor_copy(ltok[:, tt * 8:(tt + 1) * 8], sl), "sv")
    O("v", lambda e: e.tensor_reduce(stok[:, 0:8], ltok[:].rearrange("p (t e) -> p t e", e=8), AX.X, ALU.max))
    v_ng = O("v", lambda e: e.tensor_scalar(stok[:, 8:16], stok[:, 0:8], -1.0, None, ALU.mult), "sv")
    W("s", "sv", v_ng)
    s_ex = None
    for tt in range(8):
        s_ex = O("s", lambda e, tt=tt: e.activation(
            etok[:, tt * 8:(tt + 1) * 8], ltok[:, tt * 8:(tt + 1) * 8], AF.Exp,
            bias=stok[:, 8 + tt:8 + tt + 1]), "ss")
    W("v", "ss", s_ex)
    O("v", lambda e: e.tensor_reduce(stok[:, 16:24], etok[:].rearrange("p (t e) -> p t e", e=8), AX.X, ALU.add))
    O("v", lambda e: e.tensor_reduce(stok[:, 24:32], etok[:].rearrange("p (t e) -> p t e", e=8), AX.X, ALU.max))
    O("v", lambda e: e.reciprocal(stok[:, 40:48], stok[:, 16:24]))
    v_c = None
    for tt in range(8):
        esl = etok[:, tt * 8:(tt + 1) * 8]
        O("v", lambda e, esl=esl, tt=tt: e.tensor_scalar(mtmp[:, 0:8], esl, stok[:, 24 + tt:25 + tt], None, ALU.is_equal))
        O("v", lambda e, esl=esl: e.scalar_tensor_tensor(mtmp[:, 8:16], mtmp[:, 0:8], -1e30, esl, ALU.mult, ALU.add))
        O("v", lambda e, tt=tt: e.tensor_reduce(stok[:, 32 + tt:33 + tt], mtmp[:, 8:16], AX.X, ALU.max))
        O("v", lambda e, esl=esl, tt=tt: e.tensor_scalar(mtmp[:, 16:24], esl, stok[:, 32 + tt:33 + tt], None, ALU.is_ge))
        O("v", lambda e, esl=esl, tt=tt: e.tensor_scalar(mtmp[:, 24:32], esl, stok[:, 40 + tt:41 + tt], None, ALU.mult))
        v_c = O("v", lambda e, tt=tt: e.tensor_tensor(
            ctok[:, tt * 8:(tt + 1) * 8], mtmp[:, 24:32], mtmp[:, 16:24], ALU.mult), "sv")
    W("t", "sv", v_c)
    v_cr = None
    v_cr_slot = [None, None]
    for tt in range(8):
        sl = pZ[0:8, 512 * (tt % 2) + 128:512 * (tt % 2) + 256]
        W("t", "sv", v_cr_slot[tt % 2])
        ttr = O("t", lambda e, tt=tt, sl=sl: e.transpose(sl, ctok[:, tt * 8:(tt + 1) * 8], idf[:]), "st")
        W("v", "st", ttr)
        v_cr = v_cr_slot[tt % 2] = O("v", lambda e, tt=tt, sl=sl: e.tensor_copy(crow[:, tt * 128:(tt + 1) * 128], sl), "sv")
    p.flush()

    if ckpt("router", [("crow", crow[:, :]), ("ctok", ctok[:]), ("etok", etok[:])]):
        return nc, taps

    GU = [wb[:, 0:2048], wb[:, 2048:4096], wb[:, 4096:6144], wb[:, 6144:8192]]
    WD = [wb[:, 8192:9600], wb[:, 9600:11008]]
    dw_gu = [None, None]
    fin_gu = [None, None]
    dw_wd = [None, None]
    fin_wd = [None, None]

    def load_gu(pair, gsrc, usrc, waitfin):
        W("g", "st", waitfin)
        a = O("g", lambda e, pair=pair, gsrc=gsrc: e.dma_start(
            out=GU[2 * pair], in_=gsrc.rearrange("p j c -> p (j c)")), "dw")
        b = O("g", lambda e, pair=pair, usrc=usrc: e.dma_start(
            out=GU[2 * pair + 1], in_=usrc.rearrange("p j c -> p (j c)")), "dw")
        dw_gu[pair] = b
        return b

    def gate_up_act(e_idx, ic, pair, comb):
        par = ic % 2
        pG = (pS0, pO)[par]
        pU = (pS1, pZ)[par]
        W("t", "dw", dw_gu[pair])
        W("t", "ss", p._sig2[par])
        tg = tu = None
        for nt in range(2):
            for jc in range(NJ):
                tg = O("t", lambda e, nt=nt, jc=jc, pair=pair, pG=pG: e.matmul(
                    pG[:, nt * 512:(nt + 1) * 512], GU[2 * pair][:, jc * 128:(jc + 1) * 128],
                    h2[:, jc * 1024 + nt * 512: jc * 1024 + (nt + 1) * 512],
                    start=(jc == 0), stop=(jc == NJ - 1)), "st")
        W("t", "sv", p._he2[par])
        for nt in range(2):
            for jc in range(NJ):
                tu = O("t", lambda e, nt=nt, jc=jc, pair=pair, pU=pU: e.matmul(
                    pU[:, nt * 512:(nt + 1) * 512], GU[2 * pair + 1][:, jc * 128:(jc + 1) * 128],
                    h2[:, jc * 1024 + nt * 512: jc * 1024 + (nt + 1) * 512],
                    start=(jc == 0), stop=(jc == NJ - 1)), "st")
        fin_gu[pair] = tu
        W("s", "st", tg)
        W("s", "sv", p._he)
        p._sig = p._sig2[par] = O("s", lambda e, pG=pG: e.activation(
            sgact[:], pG[:, 0:1024], AF.Sigmoid), "ss")
        W("v", "ss", p._sig)
        W("v", "st", tu)
        he = O("v", lambda e, ic=ic, pU=pU: e.scalar_tensor_tensor(
            hebuf[:, ic * 1024:(ic + 1) * 1024], pU[:, 0:1024], 1.0, sgact[:],
            ALU.mult, ALU.mult), "sv")
        if comb is not None:
            he = O("v", lambda e, ic=ic: e.tensor_tensor(
                hebuf[:, ic * 1024:(ic + 1) * 1024], hebuf[:, ic * 1024:(ic + 1) * 1024],
                cbcst[:], ALU.mult), "sv")
        p._he = p._he2[par] = he
        return he

    def down_proj(wd_src_fn, he_ready, tag):
        last_ev = None
        for mc in range(NM):
            s_ = mc % 2
            W("g", "st", fin_wd[s_])
            dw_wd[s_] = O("g", lambda e, s_=s_, mc=mc: e.dma_start(
                out=WD[s_], in_=wd_src_fn(mc).rearrange("p j c -> p (j c)")), "dw")
            W("t", "dw", dw_wd[s_])
            W("t", "sv", he_ready)
            for nt in range(2):
                sl, i6 = pslice()
                evp = ev_ps[i6]
                if isinstance(evp, tuple):
                    W("t", evp[0], evp[1])
                else:
                    W("t", "sv", evp)
                for ic in range(NIC):
                    tm = O("t", lambda e, s_=s_, ic=ic, nt=nt, sl=sl: e.matmul(
                        sl, WD[s_][:, ic * 128:(ic + 1) * 128],
                        hebuf[:, ic * 1024 + nt * 512: ic * 1024 + (nt + 1) * 512],
                        start=(ic == 0), stop=(ic == NIC - 1)), "st")
                fin_wd[s_] = tm
                W("v", "st", tm)
                ev = O("v", lambda e, sl=sl, mc=mc, nt=nt: e.scalar_tensor_tensor(
                    acc[:, mc * 1024 + nt * 512: mc * 1024 + (nt + 1) * 512], sl,
                    msl(mgbc)[:, mc:mc + 1],
                    acc[:, mc * 1024 + nt * 512: mc * 1024 + (nt + 1) * 512],
                    ALU.mult, ALU.add), "sv")
                ev_ps[i6] = ev
                last_ev = ev
        return last_ev

    p.bar("t")
    W("g", "sv", p.cnt["sv"])
    W("g", "dx", dx_x1_all)
    W("g", "st", fin_wo[0])
    W("g", "st", fin_wo[1])
    W("t", "sv", p.cnt["sv"])
    W("t", "ss", p.cnt["ss"])
    load_gu(0, sgw[0], suw[0], fin_wo[0])
    he_last = None
    for ic in range(NIC):
        pair = ic % 2
        if ic + 1 < NIC:
            load_gu((ic + 1) % 2, sgw[ic + 1], suw[ic + 1], fin_gu[(ic + 1) % 2])
        he_last = gate_up_act(-1, ic, pair, None)
    down_proj(lambda mc: sdw[mc], he_last, "sd")

    if ckpt("shared", [("accS", acc[:]), ("he", hebuf[:])]):
        return nc, taps

    v_cb = None
    for e_i in range(NE):
        W("g", "sv", v_cr)
        W("g", "st", p.cnt["st"] if e_i > 0 else 0)
        s_cb = O("g", lambda e, e_i=e_i: e.dma_start(
            out=zrow[0:1, 0:1024], in_=crow[e_i:e_i + 1, :]), "dw")
        W("t", "dw", s_cb)
        W("t", "sv", v_cb)
        for nt in range(2):
            tm = O("t", lambda e, nt=nt: e.matmul(
                pZ[:, nt * 512:(nt + 1) * 512], onesr[:], zrow[0:1, nt * 512:(nt + 1) * 512],
                start=True, stop=True), "st")
            W("v", "st", tm)
            v_cb = O("v", lambda e, nt=nt: e.tensor_copy(
                cbcst[:, nt * 512:(nt + 1) * 512], pZ[:, nt * 512:(nt + 1) * 512]), "sv")
        p._he2[1] = max(p._he2[1] or 0, v_cb) or v_cb
        W("v", "sv", v_cb)
        load_gu(0, wgw[e_i, 0], wuw[e_i, 0], fin_gu[0])
        he_last = None
        for ic in range(NIC):
            pair = ic % 2
            if ic + 1 < NIC:
                load_gu((ic + 1) % 2, wgw[e_i, ic + 1], wuw[e_i, ic + 1], fin_gu[(ic + 1) % 2])
            elif e_i + 1 < NE:
                load_gu((ic + 1) % 2, wgw[e_i + 1, 0], wuw[e_i + 1, 0], fin_gu[(ic + 1) % 2])
            he_last = gate_up_act(e_i, ic, pair, True)
        last_ev = down_proj(lambda mc, e_i=e_i: wdw[e_i, mc], he_last, "wd%d" % e_i)
        if e_i == 0 and ckpt("exp0", [("accE", acc[:]), ("heE", hebuf[:]), ("cbc0", cbcst[:])]):
            return nc, taps
    p.flush()

    W("y", "sv", last_ev)
    dx_out = None
    for mc in range(NM):
        dx_out = O("y", lambda e, mc=mc: e.dma_start(
            out=out[mc * 128:(mc + 1) * 128, :], in_=acc[:, mc * 1024:(mc + 1) * 1024]), "dx")
    W("y", "dx", dx_out)
    p.flush()

    esC.close()
    es.close()
    return nc, dbg



def _tile_w(WT, ncol):
    nin = WT.shape[0] // 128
    o = WT.shape[1]
    return np.ascontiguousarray(
        WT.reshape(nin, 128, o // ncol, ncol).transpose(2, 1, 0, 3)).astype(np.float32)


def prep_in_maps(inp):
    f = lambda k: np.asarray(inp[k], dtype=np.float32)
    x = f("L0_attn_hidden_states")
    Wq, Wk, Wv, Wo = f("L0_attn_Wq"), f("L0_attn_Wk"), f("L0_attn_Wv"), f("L0_attn_Wo")
    gamma, beta = f("L0_attn_gamma"), f("L0_attn_beta")
    rwm = f("L0_ffn_router_w")
    sgm, sum_, sdm = f("L0_ffn_shared_gate"), f("L0_ffn_shared_up"), f("L0_ffn_shared_down")
    wgm, wum, wdm = f("L0_ffn_w_gate"), f("L0_ffn_w_up"), f("L0_ffn_w_down")
    cm, wmodm = f("L0_c"), f("L0_wmod")

    shared = dict(
        wq=_tile_w(Wq.T, 512), wk=_tile_w(Wk.T, 512), wv=_tile_w(Wv.T, 512),
        wo=_tile_w(Wo.T, 128), wmod=_tile_w(wmodm, 512),
        sgw=_tile_w(sgm.T, 128), suw=_tile_w(sum_.T, 128), sdw=_tile_w(sdm.T, 128),
        wgw=np.stack([_tile_w(wgm[e].T, 128) for e in range(NE)]),
        wuw=np.stack([_tile_w(wum[e].T, 128) for e in range(NE)]),
        wdw=np.stack([_tile_w(wdm[e].T, 128) for e in range(NE)]),
        rww=np.ascontiguousarray(rwm.T.reshape(16, 128, 8).transpose(1, 0, 2)),
        gamma4=np.ascontiguousarray(np.broadcast_to(np.tile(gamma, 4)[None, :], (128, 512))),
        beta4=np.ascontiguousarray(np.broadcast_to(np.tile(beta, 4)[None, :], (128, 512))),
        ident=np.eye(128, dtype=np.float32),
    )

    tri = np.triu(np.ones((128, 128), np.float32))
    ones_t = np.ones((128, 128), np.float32)
    zeros_t = np.zeros((128, 128), np.float32)

    in_maps = []
    for core in range(8):
        b, half = core // 2, core % 2
        xr = np.ascontiguousarray(x[b].T).reshape(D, 16, 128)
        perm = [j ^ half for j in range(16)]
        xT_c = np.ascontiguousarray(xr[:, perm, :].reshape(D, TB))
        xown_c = np.ascontiguousarray(xr[:, half::2, :].reshape(D, TOK))
        fm = np.stack([tri if m % 2 == 0 else (zeros_t if half == 0 else ones_t)
                       for m in range(NKT)])
        cbc_c = np.ascontiguousarray(cm[b].reshape(16, 128).T)
        m = dict(shared)
        m.update(xT=xT_c, xown=xown_c, fmask=np.ascontiguousarray(fm), cbc=cbc_c)
        in_maps.append(m)
    return in_maps


def kernel(**inp):
    in_maps = prep_in_maps(inp)
    nc, _ = build_program()
    res = run_bass_kernel_spmd(nc, in_maps, core_ids=list(range(8)))

    full = np.empty((4, TB, D), np.float32)
    for core in range(8):
        b, half = core // 2, core % 2
        o = res.results[core]["out"]
        full[b].reshape(16, 128, D)[half::2] = o.T.reshape(NP, 128, D)
    return full


# version 06 (final)
# speedup vs baseline: 1.2279x; 1.1710x over previous
import numpy as np
from contextlib import ExitStack

import concourse.bass as bass
import concourse.mybir as mybir
from concourse.bass_utils import run_bass_kernel_spmd

F32 = mybir.dt.float32
BF16 = mybir.dt.bfloat16
AF = mybir.ActivationFunctionType
ALU = mybir.AluOpType
AX = mybir.AxisListType

D = 2048
TB = 2048
TOK = 1024
NJ = 16
NP = 8
NKT = 16
H = 16
HD = 128
NG = 4
FI = 1408
NIC = 11
NE = 8
NM = 16
EPS = 1e-5

ENG = {"t": "tensor", "v": "vector", "s": "scalar", "g": "gpsimd", "y": "sync"}
DBG = {}
STOP = None
MMAX = 16


class Pr:
    def __init__(self, nc, block, sems):
        self.nc, self.block, self.sems = nc, block, sems
        self.pend = {k: [] for k in ENG}
        self.cnt = {k: 0 for k in sems}
        self.lastw = {}

    def op(self, eng, fn, inc=None):
        amt = 0
        if inc is not None:
            amt = 16 if inc in ("dw", "dx") else 1
            self.cnt[inc] += amt
        sems = self.sems
        dr = eng == "v"

        def run(e, fn=fn, inc=inc, amt=amt, dr=dr):
            if dr:
                e.drain()
            r = fn(e)
            if inc is not None:
                r.then_inc(sems[inc], amt)

        self.pend[eng].append(run)
        return self.cnt[inc] if inc is not None else None

    def wait(self, eng, sem, val):
        if val is None or val <= 0:
            return
        key = (eng, sem)
        if self.lastw.get(key, 0) >= val:
            return
        self.lastw[key] = val
        sems = self.sems
        self.pend[eng].append(lambda e, sem=sem, val=val: e.wait_ge(sems[sem], val))

    def bar(self, eng, srcs=("sv", "ss")):
        for s_ in srcs:
            self.wait(eng, s_, self.cnt[s_])

    def flush(self):
        for eng, lst in self.pend.items():
            if not lst:
                continue

            def mk(lst):
                def f(e):
                    for fn in lst:
                        fn(e)
                return f

            getattr(self.block, ENG[eng])(mk(lst))
            self.pend[eng] = []


def build_program():
    nc = bass.Bass("TRN2", target_bir_lowering=False)
    dram = lambda n, s, ty=F32, k="ExternalInput": nc.dram_tensor(n, s, ty, kind=k).ap()

    xT = dram("xT", [D, TB])
    xown = dram("xown", [D, TOK])
    wq = dram("wq", [NG, 128, NJ, 512])
    wk = dram("wk", [NG, 128, NJ, 512])
    wv = dram("wv", [NG, 128, NJ, 512])
    wo = dram("wo", [NM, 128, NJ, 128])
    wmod = dram("wmod", [24, 128, NJ, 512])
    cbc = dram("cbc", [128, NJ])
    sgw = dram("sgw", [NIC, 128, NJ, 128])
    suw = dram("suw", [NIC, 128, NJ, 128])
    sdw = dram("sdw", [NM, 128, NIC, 128])
    wgw = dram("wgw", [NE, NIC, 128, NJ, 128])
    wuw = dram("wuw", [NE, NIC, 128, NJ, 128])
    wdw = dram("wdw", [NE, NM, 128, NIC, 128])
    rww = dram("rww", [128, NJ, 8])
    gamma4 = dram("gamma4", [128, 512])
    beta4 = dram("beta4", [128, 512])
    fmask = dram("fmask", [2, 128, 128])
    ident = dram("ident", [128, 128])
    out = dram("out", [D, TOK], F32, "ExternalOutput")
    x1s = dram("x1s", [D, TOK], F32, "Internal")
    dbg = {k: dram("dbg_" + k, list(s), F32, "ExternalOutput") for k, s in DBG.items()}

    es = ExitStack()
    sb = lambda n, s, ty: es.enter_context(nc.sbuf_tensor(n, s, ty))
    psal = lambda n: es.enter_context(nc.psum_tensor(n, [128, 1024], F32))

    gam4 = sb("gam4", [128, 512], BF16)
    bet4 = sb("bet4", [128, 512], F32)
    fmk = sb("fmk", [128, 2 * 128], BF16)
    idb = sb("idb", [128, 128], BF16)
    idf = sb("idf", [128, 128], F32)
    onesc = sb("onesc", [128, 1], BF16)
    onesr = sb("onesr", [1, 128], BF16)
    modc = sb("modc", [128, 96], F32)
    cbs = sb("cbs", [128, NJ], F32)
    silc = sb("silc", [128, NJ], BF16)
    mder = sb("mder", [128, 96], F32)
    mtmp = sb("mtmp", [128, 48], F32)
    rowA = sb("rowA", [32, 1024], F32)
    rowB = sb("rowB", [32, 1024], F32)
    fmst = sb("fmst", [128, 384], F32)
    junk = sb("junk", [128, 128], BF16)
    arow = sb("arow", [1, 512], BF16)
    brow = sb("brow", [1, 512], BF16)
    zrow = sb("zrow", [1, 1024], BF16)
    mrow = sb("mrow", [1, 1024], BF16)
    wmb = sb("wmb", [128, 1024], BF16)
    betar = sb("betar", [1, 512], BF16)
    rwsb = sb("rwsb", [128, NJ * 8], BF16)
    wb = sb("wb", [128, 16384], BF16)
    wbf = wb[:].bitcast(F32)

    pO = psal("pO")
    pS0 = psal("pS0")
    pS1 = psal("pS1")
    pZ = psal("pZ")

    esH = ExitStack()
    h1 = esH.enter_context(nc.sbuf_tensor("h1", [128, NJ * TB], BF16))
    esA = ExitStack()
    sbA = lambda n, s, ty: esA.enter_context(nc.sbuf_tensor(n, s, ty))
    xbf = sbA("xbf", [128, NJ * TB], BF16)
    sct = sbA("sct", [128, 1024], BF16)
    abc1 = sbA("abc1", [128, TB], BF16)
    bbc1 = sbA("bbc1", [128, TB], BF16)

    sems = {k: es.enter_context(nc.semaphore(k)) for k in ("st", "sv", "ss", "dw", "dx", "sg")}
    block = es.enter_context(nc.Block())
    p = Pr(nc, block, sems)
    p._sig = None
    p._he = None
    p._trev = [None, None, None]
    p._sig2 = [None, None]
    p._he2 = [None, None]
    W, O = p.wait, p.op

    taps = {}

    def ckpt(name, tap_list):
        if STOP != name:
            return False
        for tag, ap in tap_list:
            t = nc.dram_tensor("tap_" + tag, [ap.shape[0], ap.shape[1]], ap.dtype,
                               kind="ExternalOutput").ap()
            p.bar("y", ("sv", "ss", "st"))
            O("y", lambda e, t=t, ap=ap: e.dma_start(out=t, in_=ap), "dx")
            taps[tag] = t
        W("y", "dx", p.cnt["dx"])
        p.flush()
        return True

    SA, CA, GA, SBv, CB, GB = range(6)
    mg1, mg2, mgac, mgbc, mgbg2, mgbs2 = range(6)
    msl = lambda i: mder[:, i * 16:(i + 1) * 16]

    c_bet4 = O("y", lambda e: e.dma_start(out=bet4[:], in_=beta4[:]), "dx")
    c_idf = O("y", lambda e: e.dma_start(out=idf[:], in_=ident[:]), "dx")
    c_cbs = O("y", lambda e: e.dma_start(out=cbs[:], in_=cbc[:]), "dx")
    c_gam = O("g", lambda e: e.dma_start(out=gam4[:], in_=gamma4[:]), "dw")
    c_idb = O("g", lambda e: e.dma_start(out=idb[:], in_=ident[:]), "dw")
    c_rw = O("g", lambda e: e.dma_start(out=rwsb[:], in_=rww[:].rearrange("p j e -> p (j e)")), "dw")
    c_fmk = None
    for m in range(2):
        c_fmk = O("g", lambda e, m=m: e.dma_start(out=fmk[:, m * 128:(m + 1) * 128], in_=fmask[m]), "dw")
    O("v", lambda e: e.memset(onesc[:], 1.0))
    O("v", lambda e: e.memset(onesr[:], 1.0))
    O("v", lambda e: e.memset(rowA[:], 0.0))
    O("v", lambda e: e.memset(rowB[:], 0.0), "sv")
    W("s", "dx", c_cbs)
    s_sil = O("s", lambda e: e.activation(silc[:], cbs[:], AF.Silu), "ss")
    W("s", "dx", c_bet4)
    s_betar = O("s", lambda e: e.copy(betar[0:1, 0:512], bet4[0:1, 0:512]), "ss")

    dw_wm = [None] * 24
    dw_x = [None] * NJ
    wm_fin = [None] * 24

    def dma_wm(nb):
        slot = wb[:, (nb % 2) * 8192:(nb % 2) * 8192 + 8192]
        if nb >= 2:
            W("g", "st", wm_fin[nb - 2])
        dw_wm[nb] = O("g", lambda e, nb=nb, slot=slot: e.dma_start(
            out=slot, in_=wmod[nb].rearrange("p j c -> p (j c)")), "dw")

    dma_wm(0)
    dma_wm(1)
    for i in range(NJ):
        dw_x[i] = O("g", lambda e, i=i: e.dma_start(
            out=xbf[:, i * TB:(i + 1) * TB], in_=xT[i * 128:(i + 1) * 128, :]), "dw")

    xsq = sct
    mod_evac = [None] * 24
    col_fin = [None] * 24
    st_stats = None
    s_xsq = None

    def emit_mod_block(nb):
        slot = wb[:, (nb % 2) * 8192:(nb % 2) * 8192 + 8192]
        bank = pO if nb % 2 == 0 else pS0
        prow = bank[0:1, 0:512]
        W("t", "dw", dw_wm[nb])
        W("t", "ss", s_sil)
        if nb >= 2:
            W("t", "sv", mod_evac[nb - 2])
        for jc in range(NJ):
            wm_fin[nb] = O("t", lambda e, jc=jc, slot=slot, prow=prow: e.matmul(
                prow, silc[:, jc:jc + 1], slot[:, jc * 512:(jc + 1) * 512],
                start=(jc == 0), stop=(jc == NJ - 1)), "st")
        W("v", "st", wm_fin[nb])
        if nb >= 2:
            W("v", "st", col_fin[nb - 2])
        mod_evac[nb] = O("v", lambda e, nb=nb, prow=prow: e.tensor_copy(
            mrow[0:1, (nb % 2) * 512:(nb % 2) * 512 + 512], prow), "sv")
        W("t", "sv", mod_evac[nb])
        v_mcol = None
        for jcl in range(4):
            col = (nb // 4) * 16 + (nb % 4) * 4 + jcl
            pcol = bank[:, 520 + jcl:521 + jcl]
            if nb >= 2:
                W("t", "sv", mod_evac[nb - 2])
            col_fin[nb] = O("t", lambda e, nb=nb, jcl=jcl, pcol=pcol: e.matmul(
                pcol, mrow[0:1, (nb % 2) * 512 + jcl * 128:(nb % 2) * 512 + jcl * 128 + 128],
                onesr[0:1, 0:1], start=True, stop=True), "st")
            W("v", "st", col_fin[nb])
            v_mcol = O("v", lambda e, col=col, pcol=pcol: e.tensor_copy(
                modc[:, col:col + 1], pcol), "sv")
        mod_evac[nb] = v_mcol
        if nb + 2 < 24:
            dma_wm(nb + 2)

    for i in range(NJ):
        emit_mod_block(i)
        if i < 8:
            emit_mod_block(i + 16) if False else None
        jc = i
        for hf in range(2):
            W("s", "dw", dw_x[jc])
            if s_xsq is not None:
                W("s", "st", st_stats)
            s_xsq = O("s", lambda e, jc=jc, hf=hf: e.activation(
                xsq[:], xbf[:, jc * TB + hf * 1024: jc * TB + (hf + 1) * 1024],
                AF.Square), "ss")
            W("t", "ss", s_xsq)
            W("t", "dw", dw_x[jc])
            for nt2 in range(2):
                r0 = 32 * nt2
                c0 = hf * 512
                off = jc * TB + hf * 1024 + nt2 * 512
                O("t", lambda e, off=off, r0=r0, c0=c0, jc=jc: e.matmul(
                    pZ[r0:r0 + 1, c0:c0 + 512], onesc[:], xbf[:, off:off + 512],
                    start=(jc == 0), stop=(jc == NJ - 1)), "st")
                st_stats = O("t", lambda e, nt2=nt2, r0=r0, c0=c0, jc=jc: e.matmul(
                    pS1[r0:r0 + 1, c0:c0 + 512], onesc[:], xsq[:, nt2 * 512:(nt2 + 1) * 512],
                    start=(jc == 0), stop=(jc == NJ - 1)), "st")

    for nb in range(NJ, 24):
        emit_mod_block(nb)

    def emit_clamp(src_i, dst):
        g_ = modc[:, src_i * 16:(src_i + 1) * 16]
        O("v", lambda e, g_=g_: e.tensor_scalar(mtmp[:, 0:16], g_, 0.0, None, ALU.is_ge))
        O("v", lambda e, g_=g_: e.tensor_scalar(mtmp[:, 16:32], g_, 0.25, None, ALU.max))
        O("v", lambda e, g_=g_: e.tensor_scalar(mtmp[:, 32:48], g_, -0.25, None, ALU.min))
        O("v", lambda e: e.tensor_tensor(mtmp[:, 16:32], mtmp[:, 16:32], mtmp[:, 32:48], ALU.subtract))
        O("v", lambda e: e.tensor_tensor(mtmp[:, 16:32], mtmp[:, 16:32], mtmp[:, 0:16], ALU.mult))
        return O("v", lambda e, dst=dst: e.tensor_tensor(msl(dst), mtmp[:, 16:32], mtmp[:, 32:48], ALU.add), "sv")

    W("v", "sv", mod_evac[7])
    v_mder = O("v", lambda e: e.tensor_scalar(msl(mg1), modc[:, CA * 16:(CA + 1) * 16], 1.0, None, ALU.add), "sv")

    def row_stats_to_ab(n_tok, st_after, sumap, sqap, adst, bdst, pbank):
        W("v", "st", st_after)
        sa = sb_ = None
        last_bc = [None, None]
        last_v = None
        for k in range(n_tok // 512):
            W("v", "ss", sb_)
            O("v", lambda e, k=k: e.tensor_copy(rowA[0:1, 0:512], sumap(k)))
            O("v", lambda e, k=k: e.tensor_copy(rowA[0:1, 512:1024], sqap(k)))
            O("v", lambda e: e.transpose(rowB[:, 0:1024], rowA[:, 0:1024]))
            ss_ = lambda c: rowB[:, c:512:32]
            qs_ = lambda c: rowB[:, 512 + c:1024:32]
            iD = 1.0 / D
            O("v", lambda e, ss_=ss_, iD=iD: e.tensor_scalar(ss_(2), ss_(0), iD, None, ALU.mult))
            O("v", lambda e, ss_=ss_, qs_=qs_, iD=iD: e.tensor_scalar(ss_(3), qs_(0), iD, None, ALU.mult))
            O("v", lambda e, ss_=ss_: e.tensor_tensor(ss_(4), ss_(2), ss_(2), ALU.mult))
            O("v", lambda e, ss_=ss_: e.tensor_tensor(ss_(5), ss_(3), ss_(4), ALU.subtract))
            O("v", lambda e, ss_=ss_: e.tensor_scalar(ss_(5), ss_(5), EPS, None, ALU.add))
            O("v", lambda e, ss_=ss_: e.reciprocal(ss_(6), ss_(5)))
            vs = O("v", lambda e: e.drain(), "sv")
            W("s", "sv", vs)
            s1 = O("s", lambda e, ss_=ss_: e.activation(ss_(7), ss_(6), AF.Sqrt), "ss")
            W("v", "ss", s1)
            O("v", lambda e, ss_=ss_, qs_=qs_: e.tensor_tensor(qs_(0), ss_(2), ss_(7), ALU.mult))
            O("v", lambda e, qs_=qs_: e.tensor_scalar(qs_(0), qs_(0), -1.0, None, ALU.mult))
            O("v", lambda e, ss_=ss_: e.tensor_scalar(ss_(0), ss_(7), 1.0, None, ALU.mult))
            v2 = O("v", lambda e: e.transpose(rowA[:, 0:1024], rowB[:, 0:1024]), "sv")
            W("s", "sv", v2)
            W("s", "st", last_bc[0])
            sa = O("s", lambda e: e.copy(arow[0:1, 0:512], rowA[0:1, 0:512]), "ss")
            sb_ = O("s", lambda e: e.copy(brow[0:1, 0:512], rowA[0:1, 512:1024]), "ss")
            W("t", "ss", sb_)
            for which, row_t, dst in ((0, arow, adst), (1, brow, bdst)):
                col = pbank[:, (which % 2) * 512:(which % 2) * 512 + 512]
                W("t", "sv", last_bc[1] if False else None)
                tm = O("t", lambda e, col=col, row_t=row_t: e.matmul(
                    col, onesr[:], row_t[0:1, 0:512], start=True, stop=True), "st")
                last_bc[0] = tm
                W("v", "st", tm)
                last_v = O("v", lambda e, col=col, dst=dst, k=k: e.tensor_copy(
                    dst[:, k * 512:(k + 1) * 512], col), "sv")
            W("t", "sv", last_v)
        return last_v

    W("t", "sv", mod_evac[7])
    vbb1 = row_stats_to_ab(
        TB, st_stats,
        lambda k: pZ[32 * (k % 2):32 * (k % 2) + 1, (k // 2) * 512:(k // 2) * 512 + 512],
        lambda k: pS1[32 * (k % 2):32 * (k % 2) + 1, (k // 2) * 512:(k // 2) * 512 + 512],
        abc1, bbc1, pO)

    W("v", "sv", vbb1)
    W("v", "sv", v_mder)
    h1_done = None
    for jc in range(NJ):
        W("v", "dw", dw_x[jc])
        for hf in range(2):
            off = jc * TB + hf * 1024
            O("v", lambda e, off=off, hf=hf: e.tensor_tensor(
                sct[:], xbf[:, off:off + 1024], abc1[:, hf * 1024:(hf + 1) * 1024], ALU.mult))
            O("v", lambda e, hf=hf: e.tensor_tensor(
                sct[:], sct[:], bbc1[:, hf * 1024:(hf + 1) * 1024], ALU.add))
            h1_done = O("v", lambda e, off=off, jc=jc: e.tensor_scalar(
                h1[:, off:off + 1024], sct[:],
                msl(mg1)[:, jc:jc + 1], modc[:, SA * 16 + jc:SA * 16 + jc + 1],
                ALU.mult, ALU.add), "sv")
    if ckpt("h1", [("h1", h1[:]), ("abc1", abc1[:]), ("bbc1", bbc1[:])]):
        return nc, taps
    p.flush()

    esA.close()
    esB = ExitStack()
    sbB = lambda n, s, ty: esB.enter_context(nc.sbuf_tensor(n, s, ty))
    osb = sbB("osb", [128, H * 1024], BF16)
    qg = sbB("qg", [128, NP * 512], BF16)
    kg = sbB("kg", [128, NKT * 512], BF16)
    vg = sbB("vg", [128, NKT * 512], BF16)
    ssq = sbB("ssq", [128, 2 * 1024], BF16)
    fqT = sbB("fqT", [128, 1024], BF16)
    fkT = sbB("fkT", [128, 2048], BF16)
    rz = sbB("rz", [128, 1024], BF16)

    QLO = [128 * (m // 2) for m in range(NKT)]
    wslot = [wb[:, 0:8192], wb[:, 8192:16384]]
    dw_w = [None, None]
    fin_w = [wm_fin[22], wm_fin[23]]
    slot_i = [0]

    def load_w(src_ap, scale_gamma):
        s_ = slot_i[0] % 2
        slot_i[0] += 1
        W("g", "st", fin_w[s_])
        dw_w[s_] = O("g", lambda e, s_=s_, src_ap=src_ap: e.dma_start(out=wslot[s_], in_=src_ap), "dw")
        vfin = None
        if scale_gamma:
            W("v", "dw", dw_w[s_])
            W("v", "dw", c_gam)
            for jc in range(NJ):
                vfin = O("v", lambda e, s_=s_, jc=jc: e.tensor_tensor(
                    wslot[s_][:, jc * 512:(jc + 1) * 512], wslot[s_][:, jc * 512:(jc + 1) * 512],
                    gam4[:], ALU.mult), "sv")
        return s_, vfin

    sQ, vQ = load_w(wq[0].rearrange("p j c -> p (j c)"), True)
    PSL = [(pO, 0), (pO, 512), (pS0, 0), (pS0, 512), (pS1, 0), (pS1, 512)]
    pi = [0]
    ev_ps = [vbb1, vbb1, None, None, None, None]

    def pslice():
        i = pi[0] % 6
        pi[0] += 1
        bank, off = PSL[i]
        return bank[:, off:off + 512], i

    st_group_end = None
    last_head_v = None
    iHD = 1.0 / HD
    MU, RS = 192, 288

    def fm_batch(c0, tiles, ss_after):
        sl_ = lambda b: fmst[:, b + c0:b + c0 + 16]
        W("v", "ss", ss_after)
        O("v", lambda e, sl_=sl_: e.tensor_scalar(sl_(MU), sl_(0), iHD, None, ALU.mult))
        O("v", lambda e, sl_=sl_: e.tensor_scalar(sl_(RS), sl_(96), iHD, None, ALU.mult))
        O("v", lambda e, sl_=sl_: e.tensor_tensor(sl_(0), sl_(MU), sl_(MU), ALU.mult))
        O("v", lambda e, sl_=sl_: e.tensor_tensor(sl_(RS), sl_(RS), sl_(0), ALU.subtract))
        O("v", lambda e, sl_=sl_: e.tensor_scalar(sl_(RS), sl_(RS), EPS, None, ALU.add))
        O("v", lambda e, sl_=sl_: e.reciprocal(sl_(96), sl_(RS)))
        vrc = O("v", lambda e: e.nop(), "sv")
        W("s", "sv", vrc)
        srs = O("s", lambda e, sl_=sl_: e.activation(sl_(RS), sl_(96), AF.Sqrt), "ss")
        W("v", "ss", srs)
        if c0 < 32:
            O("v", lambda e, sl_=sl_: e.tensor_scalar(sl_(RS), sl_(RS), HD ** -0.5, None, ALU.mult))
        vfm = None
        for dst, base, soff in tiles:
            for hh in range(4):
                vfm = O("v", lambda e, dst=dst, base=base, soff=soff, hh=hh: e.tensor_scalar(
                    dst[:, base + hh * 128: base + hh * 128 + 128],
                    dst[:, base + hh * 128: base + hh * 128 + 128],
                    fmst[:, MU + soff + hh:MU + soff + hh + 1],
                    fmst[:, RS + soff + hh:RS + soff + hh + 1],
                    ALU.subtract, ALU.mult), "sv")
        return vfm


    for g in range(NG):
        vfm = None
        sK = vK = sV = vV = None
        sK, vK = load_w(wk[g].rearrange("p j c -> p (j c)"), True)
        for kind in ("q", "k", "v"):
            if kind == "q":
                s_, vfin, ntile, dst, coloff = sQ, vQ, NP, qg, 256
            elif kind == "k":
                s_, vfin, ntile, dst, coloff = sK, vK, NKT, kg, 128
            else:
                s_, vfin, ntile, dst, coloff = sV, vV, NKT, vg, 128
            W("t", "dw", dw_w[s_])
            W("t", "sv", vfin)
            W("t", "sv", h1_done)
            for t_ in range(ntile):
                sl, i6 = pslice()
                evp = ev_ps[i6]
                if isinstance(evp, tuple):
                    W("t", evp[0], evp[1])
                else:
                    W("t", "sv", evp)
                W("t", "ss", p.cnt["ss"] if g > 0 and t_ == 0 and kind == "q" else None)
                is_v = kind == "v"
                for jc in range(NJ):
                    tm = O("t", lambda e, t_=t_, jc=jc, sl=sl, s_=s_, coloff=coloff, is_v=is_v: e.matmul(
                        sl, h1[:, jc * TB + coloff * t_: jc * TB + coloff * t_ + 128],
                        wslot[s_][:, jc * 512:(jc + 1) * 512],
                        start=(jc == 0), stop=(jc == NJ - 1 and is_v)), "st")
                if not is_v:
                    W("t", "ss", s_betar)
                    tm = O("t", lambda e, sl=sl: e.matmul(
                        sl, onesr[:], betar[0:1, 0:512], start=False, stop=True), "st")
                fin_w[s_] = tm
                W("s", "st", tm)
                W("s", "st", st_group_end)
                ev = O("s", lambda e, sl=sl, dst=dst, t_=t_: e.activation(
                    dst[:, t_ * 512:(t_ + 1) * 512], sl, AF.Copy), "ss")
                ev_ps[i6] = ("ss", ev)
                if kind != "v":
                    soff = (t_ * 4) if kind == "q" else (32 + t_ * 4)
                    W("v", "ss", ev)
                    O("v", lambda e, dst=dst, t_=t_, soff=soff: e.tensor_reduce(
                        fmst[:, soff:soff + 4],
                        dst[:, t_ * 512:(t_ + 1) * 512].rearrange("p (h d) -> p h d", d=128),
                        AX.X, ALU.add))
                    s_sq = None
                    O("s", lambda e: e.drain())
                    for hh in range(4):
                        s_sq = O("s", lambda e, dst=dst, t_=t_, hh=hh, soff=soff: e.activation(
                            junk[:, 0:128],
                            dst[:, t_ * 512 + hh * 128: t_ * 512 + hh * 128 + 128],
                            AF.Square, accum_out=fmst[:, 96 + soff + hh:96 + soff + hh + 1]),
                            "ss" if hh == 3 else None)
                    if t_ % 4 == 3:
                        c0 = soff - 12
                        tiles = [(dst, (t_ - 3 + j) * 512, c0 + 4 * j) for j in range(4)]
                        vfm = fm_batch(c0, tiles, s_sq)
            if kind == "q":
                sV, vV = load_w(wv[g].rearrange("p j c -> p (j c)"), False)
            elif kind == "k" and g < NG - 1:
                sQ, vQ = load_w(wq[g + 1].rearrange("p j c -> p (j c)"), True)

        sqm = [None, None]
        avf = [None, None]
        st_scores_last = None
        if g == 0 and ckpt("fm0", [("qg", qg[:]), ("kg", kg[:]), ("vg", vg[:]),
                                   ("fmst", fmst[:])]):
            return nc, taps
        W("t", "ss", p.cnt["ss"])
        for hh in range(4):
            h_ = g * 4 + hh
            W("t", "sv", vfm)
            W("t", "dw", c_idb)
            W("t", "ss", sqm[0])
            W("t", "ss", sqm[1])
            vtr = None
            for grp8 in range(3):
                bank, boff = ((pS0, 0), (pS1, 0), (pS0, 1024))[grp8]
                W("t", "sv", p._trev[grp8])
                ttr = None
                for j in range(8):
                    t_ = grp8 * 8 + j
                    src, base = (qg, t_ * 512) if t_ < NP else (kg, (t_ - NP) * 512)
                    sl = bank[:].bitcast(BF16)[:, boff + j * 128:boff + j * 128 + 128]
                    ttr = O("t", lambda e, src=src, base=base, hh=hh, sl=sl: e.transpose(
                        sl, src[:, base + hh * 128: base + hh * 128 + 128], idb[:]), "st")
                W("v", "st", ttr)
                W("v", "st", st_scores_last)
                dst2, doff = ((fqT, 0), (fkT, 0), (fkT, 1024))[grp8]
                full = bank[:].bitcast(BF16)[:, boff:boff + 1024]
                vtr = p._trev[grp8] = O("v", lambda e, full=full, dst2=dst2, doff=doff: e.tensor_copy(
                    dst2[:, doff:doff + 1024], full), "sv")
            if g == 0 and hh == 0 and ckpt("tr0", [("fqT", fqT[:]), ("fkT", fkT[:])]):
                return nc, taps
            def emit_scores(m):
                slot = m % 2
                pS = (pS0, pS1)[slot]
                lo = QLO[m]
                W("t", "sv", vtr)
                W("t", "ss", sqm[slot])
                blocks = [(lo, 512), (512, 1024)] if lo < 512 else [(lo, 1024)]
                tb_ = None
                for (b0, b1) in blocks:
                    tb_ = O("t", lambda e, pS=pS, b0=b0, b1=b1, m=m: e.matmul(
                        pS[:, b0:b1], fkT[:, m * 128:(m + 1) * 128], fqT[:, b0:b1],
                        start=True, stop=True), "st")
                W("s", "st", tb_)
                W("s", "st", avf[slot])
                sqm[slot] = O("s", lambda e, pS=pS, lo=lo, slot=slot: e.activation(
                    ssq[:, slot * 1024 + lo: slot * 1024 + 1024], pS[:, lo:1024], AF.Square), "ss")
                pf = m // 2
                W("g", "ss", sqm[slot])
                W("g", "dw", c_fmk)
                mkm[m] = O("g", lambda e, slot=slot, pf=pf, m=m: e.tensor_tensor(
                    ssq[:, slot * 1024 + pf * 128: slot * 1024 + pf * 128 + 128],
                    ssq[:, slot * 1024 + pf * 128: slot * 1024 + pf * 128 + 128],
                    fmk[:, (m % 2) * 128:(m % 2) * 128 + 128], ALU.mult), "sg")
                return tb_

            mkm = {}
            sq_of = {}
            st_scores_last = emit_scores(0)
            sq_of[0] = sqm[0]
            for m in range(NKT):
                slot = m % 2
                lo = QLO[m]
                if m + 1 < NKT:
                    st_scores_last = emit_scores(m + 1)
                    sq_of[m + 1] = sqm[(m + 1) % 2]
                pf = m // 2
                blocks = [(lo, 512), (512, 1024)] if lo < 512 else [(lo, 1024)]
                W("t", "ss", sq_of[m])
                for (b0, b1) in blocks:
                    if b0 <= 128 * pf < b1:
                        W("t", "sg", mkm[m])
                    O("t", lambda e, m=m, hh=hh, slot=slot, b0=b0, b1=b1: e.matmul(
                        pO[:, b0:b1], vg[:, m * 512 + hh * 128: m * 512 + hh * 128 + 128],
                        ssq[:, slot * 1024 + b0: slot * 1024 + b1],
                        start=(m == 0), stop=(m == (7 if b1 <= 512 else NKT - 1))), "st")
                    avf[slot] = O("t", lambda e, m=m, slot=slot, b0=b0, b1=b1: e.matmul(
                        pZ[0:1, b0:b1], onesc[:],
                        ssq[:, slot * 1024 + b0: slot * 1024 + b1],
                        start=(m == 0), stop=(m == (7 if b1 <= 512 else NKT - 1))), "st")
            W("v", "st", avf[0])
            W("v", "st", avf[1])
            O("v", lambda e: e.tensor_copy(rowA[0:1, 0:512], pZ[0:1, 0:512]))
            O("v", lambda e: e.tensor_copy(rowA[0:1, 512:1024], pZ[0:1, 512:1024]))
            O("v", lambda e: e.drain())
            O("v", lambda e: e.transpose(rowB[:, 0:1024], rowA[:, 0:1024]))
            zsl = lambda c: rowB[:, c:1024:32]
            O("v", lambda e, zsl=zsl: e.tensor_scalar(zsl(1), zsl(0), EPS, None, ALU.add))
            O("v", lambda e, zsl=zsl: e.reciprocal(zsl(0), zsl(1)))
            O("v", lambda e: e.drain())
            O("v", lambda e: e.transpose(rowA[:, 0:1024], rowB[:, 0:1024]))
            vz = O("v", lambda e: e.tensor_copy(zrow[0:1, 0:1024], rowA[0:1, 0:1024]), "sv")
            W("t", "sv", vz)
            W("t", "sv", last_head_v)
            for nt in range(2):
                tm = O("t", lambda e, nt=nt: e.matmul(
                    pZ[:, nt * 512:(nt + 1) * 512], onesr[:], zrow[0:1, nt * 512:(nt + 1) * 512],
                    start=True, stop=True), "st")
                W("v", "st", tm)
                O("v", lambda e, nt=nt: e.tensor_copy(rz[:, nt * 512:(nt + 1) * 512], pZ[:, nt * 512:(nt + 1) * 512]))
            for nt in range(2):
                last_head_v = O("v", lambda e, nt=nt, h_=h_: e.tensor_tensor(
                    osb[:, h_ * 1024 + nt * 512: h_ * 1024 + (nt + 1) * 512],
                    pO[:, nt * 512:(nt + 1) * 512], rz[:, nt * 512:(nt + 1) * 512], ALU.mult), "sv")
            W("t", "sv", last_head_v)
            if g == 0 and hh == 0 and ckpt("head0", [("osb0", osb[:, 0:1024]), ("fqT", fqT[:]), ("fkT", fkT[:]), ("rz", rz[:])]):
                return nc, taps
        st_group_end = p.cnt["st"]
        ev_ps[:] = [last_head_v] * 6
        p.flush()

    W("v", "sv", mod_evac[11])
    v_gac = emit_clamp(GA, mgac)
    woslot = [wb[:, 0:2048], wb[:, 2048:4096]]
    x1slot = [wbf[:, 4096:5120], wbf[:, 5120:6144]]
    dw_wo = [None, None]
    fin_wo = [fin_w[0], fin_w[1]]
    dx_x1 = [None, None]
    v_x1 = [None, None]
    for mc in range(NM):
        s_ = mc % 2
        W("g", "st", fin_wo[s_])
        dw_wo[s_] = O("g", lambda e, mc=mc, s_=s_: e.dma_start(
            out=woslot[s_], in_=wo[mc].rearrange("p j c -> p (j c)")), "dw")
        W("t", "dw", dw_wo[s_])
        W("t", "sv", last_head_v)
        for nt in range(2):
            sl, i6 = pslice()
            evp = ev_ps[i6]
            if isinstance(evp, tuple):
                W("t", evp[0], evp[1])
            else:
                W("t", "sv", evp)
            W("t", "ss", p.cnt["ss"] if mc == 0 and nt == 0 else None)
            for ic in range(NJ):
                tm = O("t", lambda e, s_=s_, ic=ic, nt=nt, sl=sl: e.matmul(
                    sl, woslot[s_][:, ic * 128:(ic + 1) * 128],
                    osb[:, ic * 1024 + nt * 512: ic * 1024 + (nt + 1) * 512],
                    start=(ic == 0), stop=(ic == NJ - 1)), "st")
            fin_wo[s_] = tm
            W("v", "st", tm)
            W("v", "dx", dx_x1[s_])
            W("v", "sv", v_gac)
            ev = O("v", lambda e, sl=sl, s_=s_, nt=nt, mc=mc: e.tensor_scalar(
                x1slot[s_][:, nt * 512:(nt + 1) * 512], sl,
                msl(mgac)[:, mc:mc + 1], None, ALU.mult), "sv")
            ev_ps[i6] = ev
        W("y", "sv", ev)
        dx_x1[s_] = O("y", lambda e, mc=mc, s_=s_: e.dma_start(
            out=x1s[mc * 128:(mc + 1) * 128, :], in_=x1slot[s_][:, 0:1024]), "dx")
    dx_x1_all = p.cnt["dx"]
    if ckpt("oproj", [("osb", osb[:])]):
        return nc, taps
    p.flush()

    esB.close()
    esH.close()
    esC = ExitStack()
    sbC = lambda n, s, ty: esC.enter_context(nc.sbuf_tensor(n, s, ty))
    acc = sbC("acc", [128, NM * 1024], F32)
    h2 = sbC("h2", [128, NJ * 1024], BF16)
    hebuf = sbC("hebuf", [128, NIC * 1024], BF16)
    sgact = sbC("sgact", [128, 1024], F32)
    tf1 = sbC("tf1", [128, 1024], F32)
    abc2 = sbC("abc2", [128, 1024], F32)
    bbc2 = sbC("bbc2", [128, 1024], F32)
    ltok = sbC("ltok", [128, 64], F32)
    etok = sbC("etok", [128, 64], F32)
    ctok = sbC("ctok", [128, 64], F32)
    lgr = rowA[0:8, 0:1024]
    crow = rowB[0:8, 0:1024]
    stok = sbC("stok", [128, 64], F32)
    cbcst = sbC("cbcst", [128, 1024], BF16)

    W("v", "sv", mod_evac[23])
    O("v", lambda e: e.tensor_scalar(msl(mg2), modc[:, CB * 16:(CB + 1) * 16], 1.0, None, ALU.add))
    emit_clamp(GB, mgbc)
    O("v", lambda e: e.tensor_tensor(msl(mgbg2), msl(mgbc), msl(mg2), ALU.mult))
    O("v", lambda e: e.tensor_tensor(msl(mgbs2), msl(mgbc), modc[:, SBv * 16:(SBv + 1) * 16], ALU.mult), "sv")

    xoslot = [wbf[:, 0:1024], wbf[:, 1024:2048]]
    x1ld = [wbf[:, 2048:3072], wbf[:, 3072:4096]]
    acc_bf = hebuf[:, 0:1024]
    acc_sq = hebuf[:, 1024:2048]
    v_acc = [None, None]
    st_l2 = None
    s_l2 = None
    W("y", "dx", dx_x1_all)
    for mc in range(NM):
        s_ = mc % 2
        W("y", "sv", v_acc[s_])
        W("y", "st", fin_wo[s_] if mc < 2 else None)
        O("y", lambda e, mc=mc, s_=s_: e.dma_start(
            out=xoslot[s_][:, 0:1024], in_=xown[mc * 128:(mc + 1) * 128, :]), "dx")
        dxv = O("y", lambda e, mc=mc, s_=s_: e.dma_start(
            out=x1ld[s_][:, 0:1024], in_=x1s[mc * 128:(mc + 1) * 128, :]), "dx")
        W("v", "dx", dxv)
        v_acc[s_] = O("v", lambda e, mc=mc, s_=s_: e.tensor_tensor(
            acc[:, mc * 1024:(mc + 1) * 1024], xoslot[s_][:, 0:1024], x1ld[s_][:, 0:1024], ALU.add), "sv")
        W("s", "sv", v_acc[s_])
        W("s", "st", st_l2)
        O("s", lambda e, mc=mc: e.activation(acc_bf, acc[:, mc * 1024:(mc + 1) * 1024], AF.Copy))
        s_l2 = O("s", lambda e, mc=mc: e.activation(acc_sq, acc[:, mc * 1024:(mc + 1) * 1024], AF.Square), "ss")
        W("t", "ss", s_l2)
        for nt in range(2):
            r0 = 32 * nt
            O("t", lambda e, nt=nt, r0=r0, mc=mc: e.matmul(
                pZ[r0:r0 + 1, 0:512], onesc[:], acc_bf[:, nt * 512:(nt + 1) * 512],
                start=(mc == 0), stop=(mc == NM - 1)), "st")
            st_l2 = O("t", lambda e, nt=nt, r0=r0, mc=mc: e.matmul(
                pZ[r0:r0 + 1, 512:1024], onesc[:], acc_sq[:, nt * 512:(nt + 1) * 512],
                start=(mc == 0), stop=(mc == NM - 1)), "st")

    p.bar("t")
    vbb2 = row_stats_to_ab(
        TOK, st_l2,
        lambda k: pZ[32 * k:32 * k + 1, 0:512],
        lambda k: pZ[32 * k:32 * k + 1, 512:1024],
        abc2, bbc2, pS0)
    ev_ps[2] = ev_ps[3] = vbb2

    if ckpt("ln2r", [("acc", acc[:])]):
        return nc, taps

    h2_ms = [None] * NM
    W("v", "sv", vbb2)
    h2g_last = None
    for mc in range(NM):
        O("v", lambda e, mc=mc: e.tensor_tensor(tf1[:], acc[:, mc * 1024:(mc + 1) * 1024], abc2[:], ALU.mult))
        O("v", lambda e: e.tensor_tensor(tf1[:], tf1[:], bbc2[:], ALU.add))
        O("v", lambda e, mc=mc: e.scalar_tensor_tensor(
            acc[:, mc * 1024:(mc + 1) * 1024], tf1[:], msl(mgbg2)[:, mc:mc + 1],
            acc[:, mc * 1024:(mc + 1) * 1024], ALU.mult, ALU.add))
        O("v", lambda e, mc=mc: e.tensor_scalar(
            acc[:, mc * 1024:(mc + 1) * 1024], acc[:, mc * 1024:(mc + 1) * 1024],
            msl(mgbs2)[:, mc:mc + 1], None, ALU.add))
        h2g_last = h2_ms[mc] = O("v", lambda e, mc=mc: e.tensor_scalar(
            h2[:, mc * 1024:(mc + 1) * 1024], tf1[:],
            msl(mg2)[:, mc:mc + 1], modc[:, SBv * 16 + mc:SBv * 16 + mc + 1],
            ALU.mult, ALU.add), "sv")
    p.flush()

    if ckpt("h2", [("h2", h2[:]), ("acc2", acc[:])]):
        return nc, taps

    W("t", "dw", c_rw)
    st_lg = None
    for jc in range(NJ):
        W("t", "sv", h2_ms[jc])
        for nt in range(2):
            st_lg = O("t", lambda e, nt=nt, jc=jc: e.matmul(
                pZ[0:8, nt * 512:(nt + 1) * 512], rwsb[:, jc * 8:(jc + 1) * 8],
                h2[:, jc * 1024 + nt * 512: jc * 1024 + (nt + 1) * 512],
                start=(jc == 0), stop=(jc == NJ - 1)), "st")
    W("v", "st", st_lg)
    v_lg = None
    for nt in range(2):
        v_lg = O("v", lambda e, nt=nt: e.tensor_copy(
            lgr[:, nt * 512:(nt + 1) * 512], pZ[0:8, nt * 512:(nt + 1) * 512]), "sv")
    W("t", "sv", v_lg)
    W("t", "dx", c_idf)
    v_lt = None
    v_lt_slot = [None, None]
    for tt in range(8):
        sl = pZ[:, 512 * (tt % 2):512 * (tt % 2) + 8]
        W("t", "sv", v_lt_slot[tt % 2])
        ttr = O("t", lambda e, tt=tt, sl=sl: e.transpose(sl, lgr[:, tt * 128:(tt + 1) * 128], idf[0:8, 0:8]), "st")
        W("v", "st", ttr)
        v_lt = v_lt_slot[tt % 2] = O("v", lambda e, tt=tt, sl=sl: e.tensor_copy(ltok[:, tt * 8:(tt + 1) * 8], sl), "sv")
    O("v", lambda e: e.tensor_reduce(stok[:, 0:8], ltok[:].rearrange("p (t e) -> p t e", e=8), AX.X, ALU.max))
    v_ng = O("v", lambda e: e.tensor_scalar(stok[:, 8:16], stok[:, 0:8], -1.0, None, ALU.mult), "sv")
    W("s", "sv", v_ng)
    s_ex = None
    for tt in range(8):
        s_ex = O("s", lambda e, tt=tt: e.activation(
            etok[:, tt * 8:(tt + 1) * 8], ltok[:, tt * 8:(tt + 1) * 8], AF.Exp,
            bias=stok[:, 8 + tt:8 + tt + 1]), "ss")
    W("v", "ss", s_ex)
    O("v", lambda e: e.tensor_reduce(stok[:, 16:24], etok[:].rearrange("p (t e) -> p t e", e=8), AX.X, ALU.add))
    O("v", lambda e: e.tensor_reduce(stok[:, 24:32], etok[:].rearrange("p (t e) -> p t e", e=8), AX.X, ALU.max))
    O("v", lambda e: e.reciprocal(stok[:, 40:48], stok[:, 16:24]))
    v_c = None
    for tt in range(8):
        esl = etok[:, tt * 8:(tt + 1) * 8]
        O("v", lambda e, esl=esl, tt=tt: e.tensor_scalar(mtmp[:, 0:8], esl, stok[:, 24 + tt:25 + tt], None, ALU.is_equal))
        O("v", lambda e, esl=esl: e.scalar_tensor_tensor(mtmp[:, 8:16], mtmp[:, 0:8], -1e30, esl, ALU.mult, ALU.add))
        O("v", lambda e, tt=tt: e.tensor_reduce(stok[:, 32 + tt:33 + tt], mtmp[:, 8:16], AX.X, ALU.max))
        O("v", lambda e, esl=esl, tt=tt: e.tensor_scalar(mtmp[:, 16:24], esl, stok[:, 32 + tt:33 + tt], None, ALU.is_ge))
        O("v", lambda e, esl=esl, tt=tt: e.tensor_scalar(mtmp[:, 24:32], esl, stok[:, 40 + tt:41 + tt], None, ALU.mult))
        v_c = O("v", lambda e, tt=tt: e.tensor_tensor(
            ctok[:, tt * 8:(tt + 1) * 8], mtmp[:, 24:32], mtmp[:, 16:24], ALU.mult), "sv")
    W("t", "sv", v_c)
    v_cr = None
    v_cr_slot = [None, None]
    for tt in range(8):
        sl = pZ[0:8, 512 * (tt % 2) + 128:512 * (tt % 2) + 256]
        W("t", "sv", v_cr_slot[tt % 2])
        ttr = O("t", lambda e, tt=tt, sl=sl: e.transpose(sl, ctok[:, tt * 8:(tt + 1) * 8], idf[:]), "st")
        W("v", "st", ttr)
        v_cr = v_cr_slot[tt % 2] = O("v", lambda e, tt=tt, sl=sl: e.tensor_copy(crow[:, tt * 128:(tt + 1) * 128], sl), "sv")
    p.flush()

    if ckpt("router", [("crow", crow[:, :]), ("ctok", ctok[:]), ("etok", etok[:])]):
        return nc, taps

    GU = [wb[:, 0:2048], wb[:, 2048:4096], wb[:, 4096:6144], wb[:, 6144:8192]]
    WD = [wb[:, 8192:9600], wb[:, 9600:11008]]
    dw_gu = [None, None]
    fin_gu = [None, None]
    dw_wd = [None, None]
    fin_wd = [None, None]

    def load_gu(pair, gsrc, usrc, waitfin):
        W("g", "st", waitfin)
        a = O("g", lambda e, pair=pair, gsrc=gsrc: e.dma_start(
            out=GU[2 * pair], in_=gsrc.rearrange("p j c -> p (j c)")), "dw")
        b = O("g", lambda e, pair=pair, usrc=usrc: e.dma_start(
            out=GU[2 * pair + 1], in_=usrc.rearrange("p j c -> p (j c)")), "dw")
        dw_gu[pair] = b
        return b

    def gate_up_act(e_idx, ic, pair, comb):
        par = ic % 2
        pG = (pS0, pO)[par]
        pU = (pS1, pZ)[par]
        W("t", "dw", dw_gu[pair])
        W("t", "ss", p._sig2[par])
        tg = tu = None
        for nt in range(2):
            for jc in range(NJ):
                tg = O("t", lambda e, nt=nt, jc=jc, pair=pair, pG=pG: e.matmul(
                    pG[:, nt * 512:(nt + 1) * 512], GU[2 * pair][:, jc * 128:(jc + 1) * 128],
                    h2[:, jc * 1024 + nt * 512: jc * 1024 + (nt + 1) * 512],
                    start=(jc == 0), stop=(jc == NJ - 1)), "st")
        W("t", "sv", p._he2[par])
        for nt in range(2):
            for jc in range(NJ):
                tu = O("t", lambda e, nt=nt, jc=jc, pair=pair, pU=pU: e.matmul(
                    pU[:, nt * 512:(nt + 1) * 512], GU[2 * pair + 1][:, jc * 128:(jc + 1) * 128],
                    h2[:, jc * 1024 + nt * 512: jc * 1024 + (nt + 1) * 512],
                    start=(jc == 0), stop=(jc == NJ - 1)), "st")
        fin_gu[pair] = tu
        W("s", "st", tg)
        W("s", "sv", p._he)
        p._sig = p._sig2[par] = O("s", lambda e, pG=pG: e.activation(
            sgact[:], pG[:, 0:1024], AF.Sigmoid), "ss")
        W("v", "ss", p._sig)
        W("v", "st", tu)
        he = O("v", lambda e, ic=ic, pU=pU: e.scalar_tensor_tensor(
            hebuf[:, ic * 1024:(ic + 1) * 1024], pU[:, 0:1024], 1.0, sgact[:],
            ALU.mult, ALU.mult), "sv")
        if comb is not None:
            he = O("v", lambda e, ic=ic: e.tensor_tensor(
                hebuf[:, ic * 1024:(ic + 1) * 1024], hebuf[:, ic * 1024:(ic + 1) * 1024],
                cbcst[:], ALU.mult), "sv")
        p._he = p._he2[par] = he
        return he

    def down_proj(wd_src_fn, he_ready, tag):
        last_ev = None
        for mc in range(NM):
            s_ = mc % 2
            W("g", "st", fin_wd[s_])
            dw_wd[s_] = O("g", lambda e, s_=s_, mc=mc: e.dma_start(
                out=WD[s_], in_=wd_src_fn(mc).rearrange("p j c -> p (j c)")), "dw")
            W("t", "dw", dw_wd[s_])
            W("t", "sv", he_ready)
            for nt in range(2):
                sl, i6 = pslice()
                evp = ev_ps[i6]
                if isinstance(evp, tuple):
                    W("t", evp[0], evp[1])
                else:
                    W("t", "sv", evp)
                for ic in range(NIC):
                    tm = O("t", lambda e, s_=s_, ic=ic, nt=nt, sl=sl: e.matmul(
                        sl, WD[s_][:, ic * 128:(ic + 1) * 128],
                        hebuf[:, ic * 1024 + nt * 512: ic * 1024 + (nt + 1) * 512],
                        start=(ic == 0), stop=(ic == NIC - 1)), "st")
                fin_wd[s_] = tm
                W("v", "st", tm)
                ev = O("v", lambda e, sl=sl, mc=mc, nt=nt: e.scalar_tensor_tensor(
                    acc[:, mc * 1024 + nt * 512: mc * 1024 + (nt + 1) * 512], sl,
                    msl(mgbc)[:, mc:mc + 1],
                    acc[:, mc * 1024 + nt * 512: mc * 1024 + (nt + 1) * 512],
                    ALU.mult, ALU.add), "sv")
                ev_ps[i6] = ev
                last_ev = ev
        return last_ev

    p.bar("t")
    W("g", "sv", p.cnt["sv"])
    W("g", "dx", dx_x1_all)
    W("g", "st", fin_wo[0])
    W("g", "st", fin_wo[1])
    W("t", "sv", p.cnt["sv"])
    W("t", "ss", p.cnt["ss"])
    load_gu(0, sgw[0], suw[0], fin_wo[0])
    he_last = None
    for ic in range(NIC):
        pair = ic % 2
        if ic + 1 < NIC:
            load_gu((ic + 1) % 2, sgw[ic + 1], suw[ic + 1], fin_gu[(ic + 1) % 2])
        he_last = gate_up_act(-1, ic, pair, None)
    down_proj(lambda mc: sdw[mc], he_last, "sd")

    if ckpt("shared", [("accS", acc[:]), ("he", hebuf[:])]):
        return nc, taps

    v_cb = None
    for e_i in range(NE):
        W("g", "sv", v_cr)
        W("g", "st", p.cnt["st"] if e_i > 0 else 0)
        s_cb = O("g", lambda e, e_i=e_i: e.dma_start(
            out=zrow[0:1, 0:1024], in_=crow[e_i:e_i + 1, :]), "dw")
        W("t", "dw", s_cb)
        W("t", "sv", v_cb)
        for nt in range(2):
            tm = O("t", lambda e, nt=nt: e.matmul(
                pZ[:, nt * 512:(nt + 1) * 512], onesr[:], zrow[0:1, nt * 512:(nt + 1) * 512],
                start=True, stop=True), "st")
            W("v", "st", tm)
            v_cb = O("v", lambda e, nt=nt: e.tensor_copy(
                cbcst[:, nt * 512:(nt + 1) * 512], pZ[:, nt * 512:(nt + 1) * 512]), "sv")
        p._he2[1] = max(p._he2[1] or 0, v_cb) or v_cb
        W("v", "sv", v_cb)
        load_gu(0, wgw[e_i, 0], wuw[e_i, 0], fin_gu[0])
        he_last = None
        for ic in range(NIC):
            pair = ic % 2
            if ic + 1 < NIC:
                load_gu((ic + 1) % 2, wgw[e_i, ic + 1], wuw[e_i, ic + 1], fin_gu[(ic + 1) % 2])
            elif e_i + 1 < NE:
                load_gu((ic + 1) % 2, wgw[e_i + 1, 0], wuw[e_i + 1, 0], fin_gu[(ic + 1) % 2])
            he_last = gate_up_act(e_i, ic, pair, True)
        last_ev = down_proj(lambda mc, e_i=e_i: wdw[e_i, mc], he_last, "wd%d" % e_i)
        if e_i == 0 and ckpt("exp0", [("accE", acc[:]), ("heE", hebuf[:]), ("cbc0", cbcst[:])]):
            return nc, taps
    p.flush()

    W("y", "sv", last_ev)
    dx_out = None
    for mc in range(NM):
        dx_out = O("y", lambda e, mc=mc: e.dma_start(
            out=out[mc * 128:(mc + 1) * 128, :], in_=acc[:, mc * 1024:(mc + 1) * 1024]), "dx")
    W("y", "dx", dx_out)
    p.flush()

    esC.close()
    es.close()
    return nc, dbg



def _tile_w(WT, ncol):
    nin = WT.shape[0] // 128
    o = WT.shape[1]
    return np.ascontiguousarray(
        WT.reshape(nin, 128, o // ncol, ncol).transpose(2, 1, 0, 3)).astype(np.float32)


def prep_in_maps(inp):
    f = lambda k: np.asarray(inp[k], dtype=np.float32)
    x = f("L0_attn_hidden_states")
    Wq, Wk, Wv, Wo = f("L0_attn_Wq"), f("L0_attn_Wk"), f("L0_attn_Wv"), f("L0_attn_Wo")
    gamma, beta = f("L0_attn_gamma"), f("L0_attn_beta")
    rwm = f("L0_ffn_router_w")
    sgm, sum_, sdm = f("L0_ffn_shared_gate"), f("L0_ffn_shared_up"), f("L0_ffn_shared_down")
    wgm, wum, wdm = f("L0_ffn_w_gate"), f("L0_ffn_w_up"), f("L0_ffn_w_down")
    cm, wmodm = f("L0_c"), f("L0_wmod")

    shared = dict(
        wq=_tile_w(Wq.T, 512), wk=_tile_w(Wk.T, 512), wv=_tile_w(Wv.T, 512),
        wo=_tile_w(Wo.T, 128), wmod=_tile_w(wmodm, 512),
        sgw=_tile_w(sgm.T, 128), suw=_tile_w(sum_.T, 128), sdw=_tile_w(sdm.T, 128),
        wgw=np.stack([_tile_w(wgm[e].T, 128) for e in range(NE)]),
        wuw=np.stack([_tile_w(wum[e].T, 128) for e in range(NE)]),
        wdw=np.stack([_tile_w(wdm[e].T, 128) for e in range(NE)]),
        rww=np.ascontiguousarray(rwm.T.reshape(16, 128, 8).transpose(1, 0, 2)),
        gamma4=np.ascontiguousarray(np.broadcast_to(np.tile(gamma, 4)[None, :], (128, 512))),
        beta4=np.ascontiguousarray(np.broadcast_to(np.tile(beta, 4)[None, :], (128, 512))),
        ident=np.eye(128, dtype=np.float32),
    )

    tri = np.triu(np.ones((128, 128), np.float32))
    ones_t = np.ones((128, 128), np.float32)
    zeros_t = np.zeros((128, 128), np.float32)

    in_maps = []
    for core in range(8):
        b, half = core // 2, core % 2
        xr = np.ascontiguousarray(x[b].T).reshape(D, 16, 128)
        perm = [j ^ half for j in range(16)]
        xT_c = np.ascontiguousarray(xr[:, perm, :].reshape(D, TB))
        xown_c = np.ascontiguousarray(xr[:, half::2, :].reshape(D, TOK))
        fm = np.stack([tri if m % 2 == 0 else (zeros_t if half == 0 else ones_t)
                       for m in range(NKT)])
        cbc_c = np.ascontiguousarray(cm[b].reshape(16, 128).T)
        m = dict(shared)
        m.update(xT=xT_c, xown=xown_c, fmask=np.ascontiguousarray(fm), cbc=cbc_c)
        in_maps.append(m)
    return in_maps


def kernel(**inp):
    in_maps = prep_in_maps(inp)
    nc, _ = build_program()
    res = run_bass_kernel_spmd(nc, in_maps, core_ids=list(range(8)))

    full = np.empty((4, TB, D), np.float32)
    for core in range(8):
        b, half = core // 2, core % 2
        o = res.results[core]["out"]
        full[b].reshape(16, 128, D)[half::2] = o.T.reshape(NP, 128, D)
    return full
